# version 13
# baseline (speedup 1.0000x reference)
import math
import os
from contextlib import ExitStack

import numpy as np
import ml_dtypes

import concourse.bass as bass
import concourse.mybir as mybir
from concourse.bass_utils import run_bass_kernel_spmd

F32 = mybir.dt.float32
BF16 = mybir.dt.bfloat16
AF = mybir.ActivationFunctionType
ALU = mybir.AluOpType
AX = mybir.AxisListType

P = 128
L = 4096
D = 2048
NT = L // P
DT = D // P
HQ, HKV, HD = 16, 4, 128
NKV = HKV * HD
E, I = 8, 1408
IT = I // P
SCALE = 1.0 / math.sqrt(HD)
QB = 512
NQB = L // QB
NQ = 4
QL = L // NQ
ROPE_THETA = 1e6
NEG = -1.0e30

DBG = bool(int(os.environ.get("MK_DBG", "0")))
NOPC = bool(int(os.environ.get("MK_NOPC", "0")))
P0LVL = int(os.environ.get("MK_P0LVL", "9"))

_CACHE = {}



def _host_consts():
    freqs = 1.0 / ROPE_THETA ** (np.arange(0, HD, 2, dtype=np.float32) / HD)
    t = np.arange(L, dtype=np.float32)
    f = np.outer(t, freqs)
    cos = np.cos(f).astype(np.float32)
    sin = np.sin(f).astype(np.float32)
    cs2 = np.concatenate([cos, cos], axis=1)
    sn2 = np.concatenate([sin, sin], axis=1)
    out = dict(
        cosq=np.ascontiguousarray(np.tile(cs2, (1, HQ))),
        sinq=np.ascontiguousarray(np.tile(sn2, (1, HQ))),
        cosk=np.ascontiguousarray(np.tile(cs2, (1, HKV))),
        sink=np.ascontiguousarray(np.tile(sn2, (1, HKV))),
        ident=np.eye(P, dtype=np.float32).astype(ml_dtypes.bfloat16),
        identf=np.eye(P, dtype=np.float32),
    )
    ii = np.arange(P)
    out["mask_lt"] = np.where(ii[None, :] <= ii[:, None], 0.0, NEG).astype(np.float32)
    out["mask_ut"] = np.where(ii[:, None] <= ii[None, :], 0.0, NEG).astype(np.float32)
    out["mask_bf"] = out["mask_lt"].astype(ml_dtypes.bfloat16)
    return out


def _prep_shared(inputs):
    c = np.ascontiguousarray
    sh = {}
    sh["AwT"] = c(inputs["L0_attn_A_w"].T)
    sh["BwT"] = c(inputs["L0_attn_B_w"].T)
    sh["CawT"] = c(inputs["L0_attn_Ca_w"].T)
    sh["CbwT"] = c(inputs["L0_attn_Cb_w"].T)
    sh["DwT"] = c(inputs["L0_attn_D_w"].T)
    sh["outwT"] = c(inputs["L0_attn_out_w"].T)
    sh["qwT"] = c(inputs["L0_ffn_q_w"].T)
    sh["kwT"] = c(inputs["L0_ffn_k_w"].T)
    sh["vwT"] = c(inputs["L0_ffn_v_w"].T)
    sh["owT"] = c(inputs["L0_ffn_o_w"].T)
    sh["gwT"] = c(inputs["L0_ffn_gate_w"].T)
    sh["wgT"] = c(inputs["L0_ffn_wg"].transpose(0, 2, 1))
    sh["wuT"] = c(inputs["L0_ffn_wu"].transpose(0, 2, 1))
    sh["wdT"] = c(inputs["L0_ffn_wd"].transpose(0, 2, 1))
    cols = lambda v: c(v.reshape(DT, P).T)
    sh["anw_c"] = cols(inputs["L0_ffn_attn_norm_w"])
    sh["fnw_c"] = cols(inputs["L0_ffn_ffn_norm_w"])
    sh["rmsw_c"] = cols(inputs["L0_attn_rms_w"])
    sh["Ab_c"] = cols(inputs["L0_attn_A_b"])
    sh["outb_r"] = c(np.broadcast_to(inputs["L0_attn_out_b"], (P, D)))
    sh["qnw_r"] = c(np.broadcast_to(np.tile(inputs["L0_ffn_q_norm_w"], HQ), (P, D)))
    sh["knw_r"] = c(np.broadcast_to(np.tile(inputs["L0_ffn_k_norm_w"], HKV), (P, NKV)))
    sh.update(_host_consts())
    return sh



class Ctx:
    pass


def declare_io(nc):
    g = Ctx()
    g.nc = nc
    dr = lambda n, s, d: nc.dram_tensor(n, list(s), d, kind="ExternalInput").ap()
    g.x = dr("x", (L, D), F32)
    for n, s in [("AwT", (D, D)), ("BwT", (D, D)), ("CawT", (D, D)),
                 ("CbwT", (D, D)), ("DwT", (D, D)), ("outwT", (D, D)),
                 ("qwT", (D, D)), ("kwT", (D, NKV)), ("vwT", (D, NKV)),
                 ("owT", (D, D)), ("gwT", (D, E)),
                 ("wgT", (E, D, I)), ("wuT", (E, D, I)), ("wdT", (E, I, D)),
                 ("anw_c", (P, DT)), ("fnw_c", (P, DT)), ("rmsw_c", (P, DT)),
                 ("Ab_c", (P, DT)), ("outb_r", (P, D)), ("qnw_r", (P, D)),
                 ("knw_r", (P, NKV)), ("cosq", (L, D)), ("sinq", (L, D)),
                 ("cosk", (L, NKV)), ("sink", (L, NKV)),
                 ("mask_lt", (P, P)), ("mask_ut", (P, P))]:
        setattr(g, n, dr(n, s, F32))
    g.ident = dr("ident", (P, P), BF16)
    g.mask_bf = nc.dram_tensor("mask_bf", [P, P], BF16, kind="ExternalInput").ap()
    g.identf = nc.dram_tensor("identf", [P, P], F32, kind="ExternalInput").ap()

    g.y = nc.dram_tensor("y", [L, D], F32, kind="ExternalOutput").ap()

    scr_kind = "ExternalOutput" if DBG else "Internal"
    scr = lambda n, s, d: nc.dram_tensor(n, list(s), d, kind=scr_kind).ap()
    g.xnbf_d = scr("xnbf_d", (L, D), BF16)
    g.attnT_d = scr("attnT_d", (HQ, P, L), BF16)
    g.qT_d = scr("qT_d", (HQ, P, L), BF16)
    g.h2T_d = scr("h2T_d", (DT, P, L), BF16)
    g.yT_d = scr("yT_d", (DT, P, L), BF16)
    g.rest_d = scr("rest_d", (L, D), F32)
    g.rest2_d = scr("rest2_d", (L, D), F32)
    g.wdbf_d = nc.dram_tensor("wdbf_d", [E, IT, P, D], BF16, kind="Internal").ap()
    if DBG:
        g.dbg_kT = scr("dbg_kT", (HKV, P, L), BF16)
        g.dbg_v = scr("dbg_v", (NT, P, NKV), BF16)
        g.dbg_qT = scr("dbg_qT", (HQ, P, L), BF16)
        g.dbg_cw = scr("dbg_cw", (NT, P, E), F32)
        g.dbg_st = scr("dbg_st", (4, P, NT), F32)
    return g


class Sem:
    def __init__(self, g, ctx, name, persist=False):
        self.h = ctx.enter_context(g.nc.semaphore(name))
        if getattr(g, "_clr", None) is not None:
            g._clr.append(self.h)


def begin_phase(g):
    g._clr = []


_CLR_N = [0]


def emit_clears(g):
    nc = g.nc
    sems = g._clr
    g._clr = None
    _CLR_N[0] += 1
    with nc.Block(f"clr{_CLR_N[0]}") as cb:
        @cb.sync
        def _(sp):
            for h in sems:
                sp.sem_clear(h)


class Chain:

    def __init__(self, g, ctx, name):
        self.h = ctx.enter_context(g.nc.semaphore(name))
        self.n = 0
        if getattr(g, "_clr", None) is not None:
            g._clr.append(self.h)

    def sync(self, eng, ins):
        self.n += 1
        ins.then_inc(self.h, 1)
        eng.wait_ge(self.h, self.n)


def wdma(eng, sem, n):
    eng.wait_ge(sem.h, 16 * n)


def transpose_blocks(t, psdst, src_bf, ident, nblk, src_off=0, dst_off=0,
                     bank_cols=512):
    ins = None
    for j in range(nblk):
        o = dst_off + j * P
        ins = t.matmul(psdst[:, o:o + P],
                       src_bf[:, src_off + j * P: src_off + (j + 1) * P],
                       ident[:, :],
                       start=(o % bank_cols == 0),
                       stop=(o % bank_cols == bank_cols - P or j == nblk - 1))
    return ins



def wload3(dram2d):
    return dram2d.rearrange("(k p) n -> p k n", p=P)


def sb3(sb_ap, k):
    return sb_ap.rearrange("p (k n) -> p k n", k=k)



def phase0(g, ctx, actx):
    nc = g.nc
    begin_phase(g)
    with ExitStack() as px:
        pe_ = px.enter_context
        sb = lambda n, s, d: pe_(nc.sbuf_tensor(n, list(s), d))
        ps = lambda n, s: pe_(nc.psum_tensor(n, list(s), F32))
        xb = [sb(f"p0_xb{i}", (P, D), F32) for i in range(4)]
        ck = [sb(f"p0_ck{i}", (P, NKV), F32) for i in range(3)]
        sk = [sb(f"p0_sk{i}", (P, NKV), F32) for i in range(3)]
        xnbf = [sb(f"p0_xnbf{i}", (P, D), BF16) for i in range(2)]
        z2bf = [sb(f"p0_z2bf{i}", (P, D), BF16) for i in range(2)]
        xnchk = sb("p0_xnchk", (P, 1), BF16)
        z2T = sb("p0_z2T", (P, D), BF16)
        kw_sb = sb("p0_kw", (P, DT * NKV), BF16)
        vw_sb = sb("p0_vw", (P, DT * NKV), BF16)
        anw = sb("p0_anw", (P, DT), F32)
        knwr = sb("p0_knwr", (P, NKV), F32)
        sq = sb("p0_sq", (P, D), F32)
        kf = sb("p0_kf", (P, NKV), F32)
        kf0 = sb("p0_kf0", (P, NKV), F32)
        krot = sb("p0_krot", (P, NKV), F32)
        kt1 = sb("p0_kt1", (P, NKV), F32)
        krope = sb("p0_krope", (P, NKV), BF16)
        w = [sb(f"p0_w{i}", (P, 1), F32) for i in range(9)]
        kms = sb("p0_kms", (P, HKV), F32)
        kms2t = sb("p0_kms2t", (P, HKV), F32)
        krms = sb("p0_krms", (P, HKV), F32)
        zsc = sb("p0_zsc", (P, 1), F32)
        zbi = sb("p0_zbi", (P, 1), F32)
        psZa = ps("p0_psZa", (P, 1024))
        psZb = ps("p0_psZb", (P, 1024))
        psk = [ps(f"p0_psk{i}", (P, NKV)) for i in range(2)]
        psv = [ps(f"p0_psv{i}", (P, NKV)) for i in range(2)]

        s_x = Sem(g, px, "p0_sx")
        s_cs = Sem(g, px, "p0_scs")
        s_w = Sem(g, ctx, "p0_sw", persist=True)
        s_vst = Sem(g, px, "p0_vst")
        s_sqr = Sem(g, px, "p0_sqr")
        s_act = Sem(g, px, "p0_act")
        s_zv = Sem(g, px, "p0_zv")
        s_wsc = Sem(g, px, "p0_wsc")
        s_zf = Sem(g, px, "p0_zf")
        s_zd = Sem(g, px, "p0_zd")
        s_kvf = Sem(g, px, "p0_kvf")
        s_sq1 = Sem(g, px, "p0_sq1")
        s_kms = Sem(g, px, "p0_kms_s")
        s_kn1 = Sem(g, px, "p0_kn1")
        s_kn2 = Sem(g, px, "p0_kn2")
        s_kn = Sem(g, px, "p0_kn")
        s_kro = Sem(g, px, "p0_kro")
        s_ktf = Sem(g, px, "p0_ktf")
        s_ktd = Sem(g, px, "p0_ktd")
        s_vd = Sem(g, px, "p0_vd")
        s_kfr = Sem(g, px, "p0_kfr")
        s_ksq = Sem(g, px, "p0_ksq")
        s_kmx = Sem(g, px, "p0_kmx")
        vch = Chain(g, px, "p0_vch")
        sch = Chain(g, px, "p0_sch")

        emit_clears(g)
        with nc.Block("p0") as b:
            @b.sync
            def _(sp):
                for t in range(NT):
                    if t >= 4:
                        sp.wait_ge(s_act.h, t - 3)
                        sp.wait_ge(s_zv.h, t - 3)
                    sp.dma_start(xb[t % 4][:], g.x[t * P:(t + 1) * P, :]).then_inc(s_x.h, 16)
                    if t >= 3 and P0LVL >= 4:
                        sp.wait_ge(s_kro.h, t - 2)
                    sp.dma_start(ck[t % 3][:], g.cosk[t * P:(t + 1) * P, :]).then_inc(s_cs.h, 16)
                    sp.dma_start(sk[t % 3][:], g.sink[t * P:(t + 1) * P, :]).then_inc(s_cs.h, 16)
                wdma(sp, g.s_xnb, NT)
                if DBG and P0LVL >= 4:
                    sp.wait_ge(s_ktd.h, NT)
                    sp.wait_ge(s_vd.h, NT)
                    sp.dma_start(g.dbg_kT.rearrange("h p l -> p h l"), g.kT_sb[:].rearrange("p (h l) -> p h l", h=HKV)).then_inc(s_x.h, 16)
                    sp.dma_start(g.dbg_v.rearrange("t p n -> p t n"), g.v_sb[:].rearrange("p (t n) -> p t n", t=NT)).then_inc(s_x.h, 16)
                    sp.dma_start(g.dbg_st[0], g.st_rstd[:]).then_inc(s_x.h, 16)
                    sp.dma_start(g.dbg_st[1], g.st_nmr[:]).then_inc(s_x.h, 16)
                    sp.dma_start(g.dbg_st[2], g.st_s[:]).then_inc(s_x.h, 16)
                    sp.dma_start(g.dbg_st[3], g.st_r[:]).then_inc(s_x.h, 16)
                    sp.wait_ge(s_x.h, 16 * (NT + 6))

            @b.gpsimd
            def _(gp):
                gp.dma_start(sb3(kw_sb[:], DT), wload3(g.kwT)).then_inc(s_w.h, 16)
                gp.dma_start(sb3(vw_sb[:], DT), wload3(g.vwT)).then_inc(s_w.h, 16)
                gp.dma_start(g.ident_sb[:], g.ident).then_inc(s_w.h, 16)
                gp.dma_start(g.identf_sb[:], g.identf).then_inc(s_w.h, 16)
                gp.dma_start(anw[:], g.anw_c).then_inc(s_w.h, 16)
                gp.dma_start(knwr[:], g.knw_r).then_inc(s_w.h, 16)

            @b.vector
            def _(v):
                wdma(v, s_w, 6)
                v.memset(g.kmax2[:], 0.0)
                v.tensor_scalar_mul(knwr[:], knwr[:], SCALE ** 0.5)
                for dt in range(DT):
                    v.tensor_scalar_mul(kw_sb[:, dt * NKV:(dt + 1) * NKV],
                                        kw_sb[:, dt * NKV:(dt + 1) * NKV],
                                        anw[:, dt:dt + 1])
                    vins = v.tensor_scalar_mul(vw_sb[:, dt * NKV:(dt + 1) * NKV],
                                               vw_sb[:, dt * NKV:(dt + 1) * NKV],
                                               anw[:, dt:dt + 1])
                vch.sync(v, vins)
                v.nop().then_inc(s_wsc.h, 1)
                for t in range(NT):
                    wdma(v, s_x, t + 1)
                    x_ = xb[t % 4]
                    vch.sync(v, v.reduce_sum(w[0][:], x_[:], axis=AX.X))
                    v.wait_ge(s_sq1.h, t + 1)
                    vch.sync(v, v.tensor_scalar_mul(w[2][:], w[0][:], 1.0 / D))
                    vch.sync(v, v.tensor_tensor(w[3][:], w[2][:], w[2][:], ALU.mult))
                    vch.sync(v, v.scalar_tensor_tensor(w[4][:], w[1][:], 1.0 / D, w[3][:],
                                                       ALU.mult, ALU.subtract))
                    vch.sync(v, v.tensor_scalar_add(w[5][:], w[4][:], 1e-5))
                    vch.sync(v, v.reciprocal(w[5][:], w[5][:]))
                    vch.sync(v, v.tensor_tensor(w[6][:], w[4][:], w[5][:], ALU.mult))
                    v.tensor_scalar_add(w[7][:], w[6][:], 1e-5)
                    vch.sync(v, v.tensor_scalar_add(w[8][:], w[6][:], 1e-6))
                    v.reciprocal(w[7][:], w[7][:])
                    vch.sync(v, v.reciprocal(w[8][:], w[8][:]))
                    v.nop().then_inc(s_vst.h, 1)
                    v.wait_ge(s_sqr.h, t + 1)
                    rstd = g.st_rstd[:, t:t + 1]
                    vch.sync(v, v.tensor_tensor(w[3][:], w[2][:], rstd, ALU.mult))
                    vch.sync(v, v.tensor_scalar_mul(g.st_nmr[:, t:t + 1], w[3][:], -1.0))
                    v.tensor_tensor(zsc[:], rstd, g.st_r[:, t:t + 1], ALU.mult)
                    vch.sync(v, v.tensor_tensor(zbi[:], g.st_nmr[:, t:t + 1],
                                                g.st_r[:, t:t + 1], ALU.mult))
                    v.nop().then_inc(s_vst.h, 1)
                    if t >= 2:
                        v.wait_ge(s_zf.h, 2 * t - 2)
                    v.tensor_scalar(z2bf[t % 2][:], x_[:], zsc[:], zbi[:],
                                    ALU.mult, ALU.add).then_inc(s_zv.h, 1)
                    if P0LVL < 3:
                        continue
                    v.wait_ge(s_kvf.h, 2 * t + 1)
                    pk = psk[t % 2]
                    v.tensor_copy(kf0[:], pk[:]).then_inc(s_kn.h, 1)
                    v.wait_ge(s_kms.h, t + 1)
                    vch.sync(v, v.tensor_scalar(kms[:], kms[:], 1.0 / HD, 1e-6,
                                                ALU.mult, ALU.add))
                    v.reciprocal(kms[:], kms[:]).then_inc(s_kn1.h, 1)
                    v.wait_ge(s_kn2.h, t + 1)
                    wdma(v, s_w, 6)
                    for j in range(HKV):
                        sl = slice(j * HD, (j + 1) * HD)
                        vins = v.scalar_tensor_tensor(kf[:, sl], kf0[:, sl], krms[:, j:j + 1],
                                                      knwr[:, sl], ALU.mult, ALU.mult)
                    if P0LVL < 4:
                        continue
                    vch.sync(v, vins)
                    kf3 = kf[:].rearrange("p (h two q) -> p h two q", h=HKV, two=2)
                    kr3 = krot[:].rearrange("p (h two q) -> p h two q", h=HKV, two=2)
                    v.tensor_scalar_mul(kr3[:, :, 0, :], kf3[:, :, 1, :],
                                        -1.0).then_inc(s_kfr.h, 1)
                    vch.sync(v, v.tensor_copy(kr3[:, :, 1, :], kf3[:, :, 0, :]))
                    wdma(v, s_cs, 2 * t + 2)
                    v.tensor_tensor(kt1[:], kf[:], ck[t % 3][:], ALU.mult)
                    vch.sync(v, v.tensor_tensor(krot[:], krot[:], sk[t % 3][:], ALU.mult))
                    v.tensor_tensor(krope[:], kt1[:], krot[:], ALU.add).then_inc(s_kro.h, 1)
                    v.wait_ge(s_ksq.h, t + 1)
                    vch.sync(v, v.tensor_tensor(g.kmax2[:], g.kmax2[:], kms2t[:],
                                                ALU.max))
                    v.nop().then_inc(s_kmx.h, 1)

            @b.scalar
            def _(sc):
                for t in range(NT):
                    wdma(sc, s_x, t + 1)
                    sc.activation(sq[:], xb[t % 4][:], AF.Square,
                                  accum_out=w[1][:]).then_inc(s_sq1.h, 1)
                    sc.wait_ge(s_vst.h, 2 * t + 1)
                    sc.activation(g.st_rstd[:, t:t + 1], w[5][:], AF.Sqrt)
                    sc.activation(g.st_r[:, t:t + 1], w[8][:], AF.Sqrt)
                    sch.sync(sc, sc.activation(g.st_s[:, t:t + 1], w[7][:], AF.Sqrt))
                    sc.nop().then_inc(s_sqr.h, 1)
                    sc.wait_ge(s_vst.h, 2 * t + 2)
                    if t >= 2:
                        wdma(sc, g.s_xnb, t - 1)
                    x_ = xb[t % 4]
                    sc.activation(xnbf[t % 2][:], x_[:], AF.Identity,
                                  bias=g.st_nmr[:, t:t + 1],
                                  scale=g.st_rstd[:, t:t + 1]).then_inc(s_act.h, 1)
                    sch.sync(sc, sc.activation(xnchk[:], xnbf[t % 2][:, 0:1], AF.Copy))
                    sc.dma_start(g.xnbf_d[t * P:(t + 1) * P, :],
                                 xnbf[t % 2][:]).then_inc(g.s_xnb.h, 16)
                    if P0LVL < 2:
                        continue
                    wdma(sc, s_w, 6)
                    sc.wait_ge(s_zf.h, 2 * t + 1)
                    for j in range(2):
                        ins = sc.activation(z2T[:, j * 512:(j + 1) * 512],
                                            psZa[:, j * 512:(j + 1) * 512], AF.Copy)
                    ins.then_inc(s_zd.h, 1)
                    sc.wait_ge(s_zf.h, 2 * t + 2)
                    for j in range(2):
                        ins = sc.activation(z2T[:, 1024 + j * 512:1024 + (j + 1) * 512],
                                            psZb[:, j * 512:(j + 1) * 512], AF.Copy)
                    ins.then_inc(s_zd.h, 1)
                    if P0LVL < 3:
                        continue
                    sc.wait_ge(s_kn.h, t + 1)
                    for j in range(HKV):
                        sl = slice(j * HD, (j + 1) * HD)
                        ins = sc.activation(sq[:, sl], kf0[:, sl], AF.Square,
                                            accum_out=kms[:, j:j + 1])
                    ins.then_inc(s_kms.h, 1)
                    sc.wait_ge(s_kn1.h, t + 1)
                    sc.activation(krms[:], kms[:], AF.Sqrt).then_inc(s_kn2.h, 1)
                    sc.wait_ge(s_kfr.h, t + 1)
                    if t >= 1:
                        sc.wait_ge(s_kmx.h, t)
                    for j in range(HKV):
                        sl = slice(j * HD, (j + 1) * HD)
                        ins = sc.activation(sq[:, sl], kf[:, sl], AF.Square,
                                            accum_out=kms2t[:, j:j + 1])
                    ins.then_inc(s_ksq.h, 1)
                    if P0LVL < 3:
                        continue
                    sc.wait_ge(s_kvf.h, 2 * t + 2)
                    sc.activation(g.v_sb[:, t * NKV:(t + 1) * NKV], psv[t % 2][:],
                                  AF.Copy).then_inc(s_vd.h, 1)
                    if P0LVL < 4:
                        continue
                    sc.wait_ge(s_ktf.h, t + 1)
                    for j in range(HKV):
                        ins = sc.activation(g.kT_sb[:, j * L + t * P: j * L + (t + 1) * P],
                                            psk[t % 2][:, j * P:(j + 1) * P], AF.Copy)
                    ins.then_inc(s_ktd.h, 1)

            @b.tensor
            def _(t_):
                wdma(t_, s_w, 6)
                t_.wait_ge(s_wsc.h, 1)
                for t in range(NT):
                    if P0LVL < 2:
                        continue
                    t_.wait_ge(s_zv.h, t + 1)
                    if t >= 1:
                        t_.wait_ge(s_zd.h, 2 * t - 1)
                    transpose_blocks(t_, psZa, z2bf[t % 2], g.ident_sb, 8,
                                     src_off=0).then_inc(s_zf.h, 1)
                    if t >= 1:
                        t_.wait_ge(s_zd.h, 2 * t)
                    transpose_blocks(t_, psZb, z2bf[t % 2], g.ident_sb, 8,
                                     src_off=8 * P).then_inc(s_zf.h, 1)
                    if P0LVL < 3:
                        continue
                    t_.wait_ge(s_zd.h, 2 * t + 2)
                    if t >= 2:
                        t_.wait_ge((s_ktd if P0LVL >= 4 else s_kn).h, t - 1)
                    for dt in range(DT):
                        ins = t_.matmul(psk[t % 2][:], z2T[:, dt * P:(dt + 1) * P],
                                        kw_sb[:, dt * NKV:(dt + 1) * NKV],
                                        start=(dt == 0), stop=(dt == DT - 1))
                    ins.then_inc(s_kvf.h, 1)
                    if t >= 2:
                        t_.wait_ge(s_vd.h, t - 1)
                    for dt in range(DT):
                        ins = t_.matmul(psv[t % 2][:], z2T[:, dt * P:(dt + 1) * P],
                                        vw_sb[:, dt * NKV:(dt + 1) * NKV],
                                        start=(dt == 0), stop=(dt == DT - 1))
                    ins.then_inc(s_kvf.h, 1)
                    if P0LVL < 4:
                        continue
                    t_.wait_ge(s_kn.h, t + 1)
                    t_.wait_ge(s_kro.h, t + 1)
                    transpose_blocks(t_, psk[t % 2], krope, g.ident_sb,
                                     4).then_inc(s_ktf.h, 1)



def build():
    nc = bass.Bass("TRN2", target_bir_lowering=False, detect_race_conditions=False)
    g = declare_io(nc)
    with ExitStack() as ctx:
        e = ctx.enter_context
        sb = lambda n, s, d: e(nc.sbuf_tensor(n, list(s), d))
        g.st_rstd = sb("st_rstd", (P, NT), F32)
        g.st_nmr = sb("st_nmr", (P, NT), F32)
        g.st_s = sb("st_s", (P, NT), F32)
        g.st_r = sb("st_r", (P, NT), F32)
        g.ident_sb = sb("ident_sb", (P, P), BF16)
        g.identf_sb = sb("identf_sb", (P, P), F32)
        g.cw_sb = sb("cw_sb", (P, NT * E), F32)
        g.kmax2 = sb("kmax2", (P, HKV), F32)
        g.qmax2 = sb("qmax2", (P, HQ), F32)
        g.negC = sb("negC", (P, 1), F32)
        g._clr = None
        g.s_xnb = Sem(g, ctx, "s_xnb")
        g.s_pcw = Sem(g, ctx, "s_pcw")

        with ExitStack() as actx:
            ae = actx.enter_context
            g.kT_sb = ae(nc.sbuf_tensor("kT_sb", [P, HKV * L], BF16))
            g.v_sb = ae(nc.sbuf_tensor("v_sb", [P, NT * NKV], BF16))
            phase0(g, ctx, actx)
            phase1(g, ctx, actx)
            phase2(g, ctx, actx)
        phase3(g, ctx)
        phase4(g, ctx)
        phase5(g, ctx)
        phase6(g, ctx)
    nc.finalize()
    return g


def phase1(g, ctx, actx):
    nc = g.nc
    begin_phase(g)
    with ExitStack() as px:
        pe_ = px.enter_context
        sb = lambda n, s, d: pe_(nc.sbuf_tensor(n, list(s), d))
        ps = lambda n, s: pe_(nc.psum_tensor(n, list(s), F32))
        xq = [sb(f"p1_xq{i}", (P, D), BF16) for i in range(2)]
        z2q = [sb(f"p1_z2q{i}", (P, D), BF16) for i in range(2)]
        z2T = sb("p1_z2T", (P, D), BF16)
        qw_sb = sb("p1_qw", (P, DT * D), BF16)
        anw = sb("p1_anw", (P, DT), F32)
        qnwr = sb("p1_qnwr", (P, D), BF16)
        cq = sb("p1_cq", (P, 1024), BF16)
        sq_ = sb("p1_sq", (P, 1024), BF16)
        qf = sb("p1_qf", (P, 1024), F32)
        qrot = sb("p1_qrot", (P, 1024), F32)
        qt1 = sb("p1_qt1", (P, 1024), F32)
        qms = sb("p1_qms", (P, DT), F32)
        qrms = sb("p1_qrms", (P, DT), F32)
        sqs = sb("p1_sqs", (P, P), F32)
        qtile = [sb(f"p1_qtile{i}", (P, D), BF16) for i in range(2)]
        qms2t = sb("p1_qms2t", (P, 8), F32)
        onesr1 = sb("p1_ones1", (1, P), BF16)
        qmxh = sb("p1_qmxh", (P, 1), F32)
        kmxh = sb("p1_kmxh", (P, 1), F32)
        c2q = sb("p1_c2q", (P, 1), F32)
        c2k = sb("p1_c2k", (P, 1), F32)
        cbf = sb("p1_cbf", (P, 1), BF16)
        psA = ps("p1_psA", (P, 1024))
        psB = ps("p1_psB", (P, 1024))

        s_xq = Sem(g, px, "p1_xq")
        s_cq = Sem(g, ctx, "p1_cq", persist=True)
        s_w1 = Sem(g, ctx, "p1_w1", persist=True)
        s_z2q = Sem(g, px, "p1_z2q")
        s_tz = Sem(g, px, "p1_tz")
        s_ze = Sem(g, px, "p1_ze")
        s_qf = Sem(g, px, "p1_qf")
        s_qms = Sem(g, px, "p1_qms")
        s_qn1 = Sem(g, px, "p1_qn1")
        s_qn2 = Sem(g, px, "p1_qn2")
        s_qd = Sem(g, px, "p1_qd")
        s_tq = Sem(g, px, "p1_tq")
        s_qe = Sem(g, px, "p1_qe")
        s_qs = Sem(g, px, "p1_qs")
        s_qfr = Sem(g, px, "p1_qfr")
        s_qsq = Sem(g, px, "p1_qsq")
        s_qmx = Sem(g, px, "p1_qmx")
        s_ca = Sem(g, px, "p1_ca")
        s_cb = Sem(g, px, "p1_cb")
        s_wsc1 = Sem(g, px, "p1_wsc")
        vch = Chain(g, px, "p1_vch")
        sch = Chain(g, px, "p1_sch")
        emit_clears(g)
        with nc.Block("p1") as b:
            @b.sync
            def _(sp):
                for tau in range(NT):
                    if tau >= 2:
                        sp.wait_ge(s_z2q.h, tau - 1)
                    sp.dma_start(xq[tau % 2][:],
                                 g.xnbf_d[tau * P:(tau + 1) * P, :]).then_inc(s_xq.h, 16)
                    sp.wait_ge(s_qe.h, 2 * tau + 2)
                    if tau >= 2:
                        wdma(sp, s_qs, tau - 1)
                    sp.dma_start(
                        g.qT_d[:, :, tau * P:(tau + 1) * P].rearrange("h p c -> p h c"),
                        qtile[tau % 2][:].rearrange("p (h c) -> p h c", h=HQ),
                    ).then_inc(s_qs.h, 16)
                wdma(sp, s_qs, NT)

            @b.gpsimd
            def _(gp):
                qw3 = wload3(g.qwT)
                qsb3 = sb3(qw_sb[:], DT)
                for q4 in range(4):
                    gp.dma_start(qsb3[:, q4 * 4:(q4 + 1) * 4, :],
                                 qw3[:, q4 * 4:(q4 + 1) * 4, :]).then_inc(s_w1.h, 16)
                gp.dma_start(anw[:], g.anw_c).then_inc(s_w1.h, 16)
                gp.dma_start(qnwr[:], g.qnw_r).then_inc(s_w1.h, 16)
                for tau in range(NT):
                    for hf in range(2):
                        step = 2 * tau + hf
                        if step >= 1:
                            gp.wait_ge(s_qd.h, step)
                        hsl0 = hf * 1024
                        gp.dma_start(cq[:], g.cosq[tau * P:(tau + 1) * P,
                                                   hsl0:hsl0 + 1024]).then_inc(s_cq.h, 16)
                        gp.dma_start(sq_[:], g.sinq[tau * P:(tau + 1) * P,
                                                    hsl0:hsl0 + 1024]).then_inc(s_cq.h, 16)

            @b.vector
            def _(v):
                wdma(v, s_w1, 6)
                v.memset(g.qmax2[:], 0.0)
                v.memset(onesr1[:], 1.0)
                v.tensor_scalar_mul(qnwr[:], qnwr[:], SCALE ** 0.5)
                for dt in range(DT):
                    vins = v.tensor_scalar_mul(qw_sb[:, dt * D:(dt + 1) * D],
                                               qw_sb[:, dt * D:(dt + 1) * D],
                                               anw[:, dt:dt + 1])
                vch.sync(v, vins)
                v.nop().then_inc(s_wsc1.h, 1)
                for tau in range(NT):
                    for hf in range(2):
                        step = 2 * tau + hf + 1
                        pshf = psA if hf == 0 else psB
                        v.wait_ge(s_qms.h, 2 * tau + hf + 1)
                        vch.sync(v, v.tensor_scalar(qrms[:, hf * 8:hf * 8 + 8],
                                                    qms[:, hf * 8:hf * 8 + 8],
                                                    1.0 / HD, 1e-6, ALU.mult, ALU.add))
                        v.reciprocal(qrms[:, hf * 8:hf * 8 + 8],
                                     qrms[:, hf * 8:hf * 8 + 8]).then_inc(s_qn1.h, 1)
                        v.wait_ge(s_qn2.h, 2 * tau + hf + 1)
                        if step >= 2:
                            v.wait_ge(s_qsq.h, step - 1)
                            hfp = (step - 2) % 2
                            vch.sync(v, v.tensor_tensor(
                                g.qmax2[:, hfp * 8:hfp * 8 + 8],
                                g.qmax2[:, hfp * 8:hfp * 8 + 8],
                                qms2t[:, 0:8], ALU.max))
                            v.nop().then_inc(s_qmx.h, 1)
                        for j in range(8):
                            sl = slice(j * P, (j + 1) * P)
                            habs = hf * 8 + j
                            vins = v.scalar_tensor_tensor(
                                qf[:, sl], pshf[:, sl], qrms[:, habs:habs + 1],
                                qnwr[:, habs * P:(habs + 1) * P], ALU.mult, ALU.mult)
                        vch.sync(v, vins)
                        qf3 = qf[:].rearrange("p (h two q) -> p h two q", h=8, two=2)
                        qr3 = qrot[:].rearrange("p (h two q) -> p h two q", h=8, two=2)
                        v.tensor_scalar_mul(qr3[:, :, 0, :], qf3[:, :, 1, :],
                                            -1.0).then_inc(s_qfr.h, 1)
                        vch.sync(v, v.tensor_copy(qr3[:, :, 1, :], qf3[:, :, 0, :]))
                        wdma(v, s_cq, 2 * (2 * tau + hf + 1))
                        hsl = slice(hf * 1024, hf * 1024 + 1024)
                        v.tensor_tensor(qt1[:], qf[:], cq[:], ALU.mult)
                        vch.sync(v, v.tensor_tensor(qrot[:], qrot[:], sq_[:], ALU.mult))
                        if tau >= 1:
                            v.wait_ge(s_tq.h, 2 * (tau - 1) + hf + 1)
                        v.tensor_tensor(z2q[tau % 2][:, hsl], qt1[:], qrot[:],
                                        ALU.add).then_inc(s_qd.h, 1)
                v.wait_ge(s_qsq.h, 2 * NT)
                vch.sync(v, v.tensor_tensor(g.qmax2[:, 8:16], g.qmax2[:, 8:16],
                                            qms2t[:, 0:8], ALU.max))
                v.nop().then_inc(s_qmx.h, 1)
                v.wait_ge(s_ca.h, 1)
                vch.sync(v, v.reduce_max(qmxh[0:16, 0:1], psA[0:16, 0:128], axis=AX.X))
                vch.sync(v, v.reduce_max(kmxh[0:4, 0:1], psB[0:4, 0:128], axis=AX.X))
                v.nop().then_inc(s_cb.h, 1)
                v.wait_ge(s_ca.h, 2)
                vch.sync(v, v.reduce_max(c2q[0:1, 0:1], psA[0:1, 0:16], axis=AX.X))
                vch.sync(v, v.reduce_max(c2k[0:1, 0:1], psB[0:1, 0:4], axis=AX.X))
                vch.sync(v, v.tensor_tensor(c2q[0:1, 0:1], c2q[0:1, 0:1],
                                            c2k[0:1, 0:1], ALU.mult))
                v.nop().then_inc(s_cb.h, 1)

            @b.scalar
            def _(sc):
                wdma(sc, s_w1, 6)
                for tau in range(NT):
                    wdma(sc, s_xq, tau + 1)
                    if tau >= 2:
                        sc.wait_ge(s_tq.h, 2 * tau - 2)
                    sc.activation(z2q[tau % 2][:], xq[tau % 2][:], AF.Identity,
                                  scale=g.st_r[:, tau:tau + 1]).then_inc(s_z2q.h, 1)
                    sc.wait_ge(s_tz.h, 2 * tau + 1)
                    for j in range(2):
                        ins = sc.activation(z2T[:, j * 512:(j + 1) * 512],
                                            psA[:, j * 512:(j + 1) * 512], AF.Copy)
                    ins.then_inc(s_ze.h, 1)
                    sc.wait_ge(s_tz.h, 2 * tau + 2)
                    for j in range(2):
                        ins = sc.activation(z2T[:, 1024 + j * 512:1024 + (j + 1) * 512],
                                            psB[:, j * 512:(j + 1) * 512], AF.Copy)
                    ins.then_inc(s_ze.h, 1)
                    for hf in range(2):
                        pshf = psA if hf == 0 else psB
                        sc.wait_ge(s_qf.h, 2 * tau + hf + 1)
                        for j in range(8):
                            ins = sc.activation(sqs[:],
                                                pshf[:, j * P:(j + 1) * P], AF.Square,
                                                accum_out=qms[:, hf * 8 + j:hf * 8 + j + 1])
                        ins.then_inc(s_qms.h, 1)
                        sc.wait_ge(s_qn1.h, 2 * tau + hf + 1)
                        sch.sync(sc, sc.activation(qrms[:, hf * 8:hf * 8 + 8],
                                                   qrms[:, hf * 8:hf * 8 + 8],
                                                   AF.Sqrt))
                        sc.nop().then_inc(s_qn2.h, 1)
                    for hf in range(2):
                        step = 2 * tau + hf + 1
                        sc.wait_ge(s_qfr.h, step)
                        if step >= 2:
                            sc.wait_ge(s_qmx.h, step - 1)
                        for j in range(8):
                            sl = slice(j * P, (j + 1) * P)
                            ins = sc.activation(sqs[:], qf[:, sl], AF.Square,
                                                accum_out=qms2t[:, j:j + 1])
                        ins.then_inc(s_qsq.h, 1)
                    for hf in range(2):
                        pshf = psA if hf == 0 else psB
                        sc.wait_ge(s_tq.h, 2 * tau + hf + 1)
                        if tau >= 2 and hf == 0:
                            wdma(sc, s_qs, tau - 1)
                        for j in range(2):
                            ins = sc.activation(
                                qtile[tau % 2][:, hf * 1024 + j * 512:
                                      hf * 1024 + (j + 1) * 512],
                                pshf[:, j * 512:(j + 1) * 512], AF.Copy)
                        ins.then_inc(s_qe.h, 1)
                sc.wait_ge(s_cb.h, 2)
                sch.sync(sc, sc.activation(cbf[0:1, 0:1], c2q[0:1, 0:1], AF.Sqrt))
                sc.nop().then_inc(s_cb.h, 1)
                sc.wait_ge(s_ca.h, 3)
                sch.sync(sc, sc.activation(g.negC[:, 0:1], psA[:, 0:1], AF.Copy,
                                           scale=-1.0))

            @b.tensor
            def _(t_):
                wdma(t_, s_w1, 6)
                t_.wait_ge(s_wsc1.h, 1)
                for tau in range(NT):
                    t_.wait_ge(s_z2q.h, tau + 1)
                    if tau >= 1:
                        t_.wait_ge(s_qe.h, 2 * tau - 1)
                    transpose_blocks(t_, psA, z2q[tau % 2], g.ident_sb, 8,
                                     src_off=0).then_inc(s_tz.h, 1)
                    if tau >= 1:
                        t_.wait_ge(s_qe.h, 2 * tau)
                    transpose_blocks(t_, psB, z2q[tau % 2], g.ident_sb, 8,
                                     src_off=8 * P).then_inc(s_tz.h, 1)
                    t_.wait_ge(s_ze.h, 2 * tau + 2)
                    for hf in range(2):
                        pshf = psA if hf == 0 else psB
                        for dt in range(DT):
                            for ch in range(2):
                                ins = t_.matmul(
                                    pshf[:, ch * 512:(ch + 1) * 512],
                                    z2T[:, dt * P:(dt + 1) * P],
                                    qw_sb[:, dt * D + hf * 1024 + ch * 512:
                                          dt * D + hf * 1024 + (ch + 1) * 512],
                                    start=(dt == 0), stop=(dt == DT - 1))
                        ins.then_inc(s_qf.h, 1)
                    for hf in range(2):
                        pshf = psA if hf == 0 else psB
                        t_.wait_ge(s_qd.h, 2 * tau + hf + 1)
                        transpose_blocks(t_, pshf, z2q[tau % 2], g.ident_sb, 8,
                                         src_off=hf * 1024).then_inc(s_tq.h, 1)
                t_.wait_ge(s_qe.h, 2 * NT)
                t_.wait_ge(s_qmx.h, 2 * NT)
                t_.matmul(psA[0:16, 0:128], g.qmax2[:], g.identf_sb[:],
                          start=True, stop=True)
                t_.matmul(psB[0:4, 0:128], g.kmax2[:], g.identf_sb[:],
                          start=True, stop=True).then_inc(s_ca.h, 1)
                t_.wait_ge(s_cb.h, 1)
                t_.matmul(psA[0:1, 0:16], qmxh[0:16, 0:1], g.identf_sb[0:16, 0:16],
                          start=True, stop=True)
                t_.matmul(psB[0:1, 0:4], kmxh[0:4, 0:1], g.identf_sb[0:4, 0:4],
                          start=True, stop=True).then_inc(s_ca.h, 1)
                t_.wait_ge(s_cb.h, 3)
                t_.matmul(psA[:, 0:1], onesr1[:], cbf[0:1, 0:1],
                          start=True, stop=True).then_inc(s_ca.h, 1)
    return


def phase2(g, ctx, actx):
    nc = g.nc
    begin_phase(g)
    with ExitStack() as px:
        pe_ = px.enter_context
        sb = lambda n, s, d: pe_(nc.sbuf_tensor(n, list(s), d))
        ps = lambda n, s: pe_(nc.psum_tensor(n, list(s), F32))
        qTb = [sb(f"p2_qTb{i}", (P, HQ * QB), BF16) for i in range(2)]
        mltb = sb("p2_mltb", (P, P), BF16)
        ones_row = sb("p2_ones", (1, P), BF16)
        p2scr = sb("p2_p2scr", (P, 4 * 2048), BF16)
        p2scrT = sb("p2_p2scrT", (P, 3 * 2048), BF16)
        l_all2 = sb("p2_lall", (P, HQ * 32), F32)
        lfin = sb("p2_lfin", (P, 4), F32)
        lbf = [sb(f"p2_lbf{i}", (P, 4), BF16) for i in range(2)]
        lrow_bf = [sb(f"p2_lrow{i}", (1, QB), BF16) for i in range(2)]
        linvb = [sb(f"p2_linvb{i}", (P, QB), BF16) for i in range(2)]
        attn_raw = [sb(f"p2_araw{i}", (P, QB), BF16) for i in range(2)]
        attn_h = [sb(f"p2_attnh{i}", (P, QB), BF16) for i in range(2)]
        pcb = [sb(f"p2_pcb{i}", (P, D), BF16) for i in range(2)]
        psA = ps("p2_psA", (P, 1024))
        psB = ps("p2_psB", (P, 1024))
        psT = [ps(f"p2_psT{i}", (P, 512)) for i in range(2)]
        psO = [ps(f"p2_psO{i}", (P, QB)) for i in range(2)]

        s_pcl = Sem(g, px, "p2_pcl")
        s_qt = Sem(g, px, "p2_qt")
        s_w2 = Sem(g, ctx, "p2_w2b", persist=True)
        s_fill = Sem(g, px, "p2_fill")
        s_mexp = Sem(g, px, "p2_mexp")
        s_trf = Sem(g, px, "p2_trf")
        s_tre = Sem(g, px, "p2_tre")
        s_pv = Sem(g, px, "p2_pv")
        s_lf = Sem(g, px, "p2_lf")
        s_ar = Sem(g, px, "p2_ar")
        s_lt = Sem(g, px, "p2_lt")
        s_lr = Sem(g, px, "p2_lr")
        s_bc = Sem(g, px, "p2_bc")
        s_li = Sem(g, px, "p2_li")
        s_ae = Sem(g, px, "p2_ae")
        s_at = Sem(g, px, "p2_at")
        vch = Chain(g, px, "p2_vch")

        NU = NQB * HQ
        sched = []
        cumF = cumT = cumC = 0
        for qb in range(NQB):
            for h in range(HQ):
                sched.append(dict(qb=qb, h=h, hx=qb * HQ + h, nch=qb + 1,
                                  bF=cumF, bT=cumT, gc0=cumC))
                cumF += 2 * (qb + 1)
                cumT += 4 * (qb + 1)
                cumC += qb + 1
        fills_thru = {}
        run = 0
        for qb in range(NQB):
            run += HQ * 2 * (qb + 1)
            fills_thru[qb] = run

        emit_clears(g)
        with nc.Block("p2") as b:
            @b.sync
            def _(sp):
                def stores(qb):
                    for h in range(HQ):
                        hx = qb * HQ + h
                        sp.wait_ge(s_ae.h, hx + 1)
                        sp.dma_start(g.attnT_d[h, :, qb * QB:(qb + 1) * QB],
                                     attn_h[hx % 2][:]).then_inc(s_at.h, 16)

                for qb in range(NQB):
                    if qb >= 2:
                        sp.wait_ge(s_fill.h, fills_thru[qb - 2])
                    sp.dma_start(
                        qTb[qb % 2][:].rearrange("p (h c) -> p h c", h=HQ),
                        g.qT_d[:, :, qb * QB:(qb + 1) * QB].rearrange("h p c -> p h c"),
                    ).then_inc(s_qt.h, 16)
                    if qb >= 1:
                        stores(qb - 1)
                stores(NQB - 1)
                wdma(sp, s_at, NU)

            @b.gpsimd
            def _(gp):
                gp.dma_start(mltb[:], g.mask_bf).then_inc(s_w2.h, 16)

                pc_ci = [0]

                def pc_chunk():
                    ci = pc_ci[0]
                    if ci >= E * IT:
                        return
                    pc_ci[0] += 1
                    e, ib = divmod(ci, IT)
                    bb = pcb[ci % 2]
                    if ci >= 2:
                        wdma(gp, g.s_pcw, ci - 1)
                    gp.dma_start(bb[:], g.wdT[e, ib * P:(ib + 1) * P, :]
                                 ).then_inc(s_pcl.h, 16)
                    wdma(gp, s_pcl, ci + 1)
                    gp.dma_start(g.wdbf_d[e, ib], bb[:]).then_inc(g.s_pcw.h, 16)

                for u in sched:
                    pc_chunk()
                    gp.wait_ge(s_mexp.h, u["bF"] + 2 * u["nch"])
                while pc_ci[0] < E * IT:
                    pc_chunk()

            @b.tensor
            def _(t_):
                wdma(t_, s_w2, 1)

                def bc(u):
                    t_.wait_ge(s_lr.h, u + 1)
                    t_.matmul(psO[u % 2][:, 0:QB], ones_row[:],
                              lrow_bf[u % 2][0:1, :],
                              start=True, stop=True).then_inc(s_bc.h, 1)

                def lbfT(u):
                    t_.wait_ge(s_lf.h, u + 1)
                    t_.wait_ge(s_ar.h, u + 1)
                    for qt in range(4):
                        ins = t_.matmul(psO[u % 2][0:1, qt * P:(qt + 1) * P],
                                        lbf[u % 2][:, qt:qt + 1], g.ident_sb[:, :],
                                        start=(qt == 0), stop=(qt == 3))
                    ins.then_inc(s_lt.h, 1)

                for u in sched:
                    qb, h, hx, nch = u["qb"], u["h"], u["hx"], u["nch"]
                    bF, bT, gc0 = u["bF"], u["bT"], u["gc0"]
                    kvh = h // 4
                    if h == 0:
                        if hx >= 2:
                            bc(hx - 2)
                        wdma(t_, s_qt, qb + 1)

                    def fills(c):
                        diag = c == qb
                        for pi in range(2):
                            gh = bF + 2 * c + pi + 1
                            if gh > 2:
                                t_.wait_ge(s_mexp.h, gh - 2)
                            pp = psA if pi == 0 else psB
                            for qi, qt in enumerate((2 * pi, 2 * pi + 1)):
                                off = qi * 512
                                qsl = qTb[qb % 2][:, h * QB + qt * P:h * QB + (qt + 1) * P]
                                kb = kvh * L + c * 512
                                if not diag:
                                    ins = t_.matmul(pp[:, off:off + 512], qsl,
                                                    g.kT_sb[:, kb:kb + 512],
                                                    start=True, stop=True)
                                else:
                                    if qt > 0:
                                        t_.matmul(pp[:, off:off + P * qt], qsl,
                                                  g.kT_sb[:, kb:kb + P * qt],
                                                  start=True, stop=True)
                                    t_.matmul(pp[:, off + P * qt:off + P * (qt + 1)],
                                              qsl,
                                              g.kT_sb[:, kb + P * qt:kb + P * (qt + 1)],
                                              start=True, stop=False)
                                    ins = t_.matmul(pp[:, off + P * qt:off + P * (qt + 1)],
                                                    g.ident_sb[:, :], mltb[:],
                                                    start=False, stop=True)
                            ins.then_inc(s_fill.h, 1)

                    def trans(c, ktl):
                        diag = c == qb
                        gt = bT + 4 * c + ktl + 1
                        gc = gc0 + c + 1
                        par = (gc - 1) % 4
                        t_.wait_ge(s_mexp.h, bF + 2 * c + 2)
                        if gt > 2:
                            t_.wait_ge(s_tre.h, gt - 2)
                        q0 = ktl if diag else 0
                        ins = None
                        for qt in range(q0, 4):
                            ins = t_.matmul(
                                psT[(gt - 1) % 2][:, qt * P:(qt + 1) * P],
                                p2scr[:, par * 2048 + qt * 512 + ktl * P:
                                      par * 2048 + qt * 512 + (ktl + 1) * P],
                                g.ident_sb[:, :], start=(qt == q0), stop=(qt == 3))
                        ins.then_inc(s_trf.h, 1)

                    first_pv = [True]

                    def pv(c, ktl):
                        diag = c == qb
                        gt = bT + 4 * c + ktl + 1
                        gc = gc0 + c + 1
                        par = (gc - 1) % 3
                        ktg = 4 * c + ktl
                        qoff = P * ktl if diag else 0
                        t_.wait_ge(s_tre.h, gt)
                        if first_pv[0] and hx >= 2:
                            t_.wait_ge(s_li.h, hx - 1)
                        last = (c == qb and ktl == 3)
                        ins = t_.matmul(
                            psO[hx % 2][:, qoff:QB],
                            g.v_sb[:, ktg * NKV + kvh * HD:ktg * NKV + (kvh + 1) * HD],
                            p2scrT[:, par * 2048 + ktl * 512 + qoff:
                                   par * 2048 + (ktl + 1) * 512],
                            start=first_pv[0], stop=last)
                        first_pv[0] = False
                        ins.then_inc(s_pv.h, 1)

                    fills(0)
                    if nch > 1:
                        fills(1)
                    if hx >= 2 and h != 0:
                        bc(hx - 2)
                    for c in range(1, nch):
                        for ktl in range(4):
                            trans(c - 1, ktl)
                            if ktl >= 1:
                                pv(c - 1, ktl - 1)
                        pv(c - 1, 3)
                        if c + 1 < nch:
                            fills(c + 1)
                    for ktl in range(4):
                        trans(nch - 1, ktl)
                        if ktl >= 1:
                            pv(nch - 1, ktl - 1)
                    pv(nch - 1, 3)
                    if hx >= 1:
                        lbfT(hx - 1)
                bc(NU - 2)
                lbfT(NU - 1)
                bc(NU - 1)

            @b.scalar
            def _(sc):
                for u in sched:
                    qb, h, hx, nch = u["qb"], u["h"], u["hx"], u["nch"]
                    bF, gc0 = u["bF"], u["gc0"]
                    la = l_all2[:, h * 32:(h + 1) * 32]
                    for c in range(nch):
                        diag = c == qb
                        gc = gc0 + c + 1
                        par = (gc - 1) % 4
                        if gc > 4:
                            sc.wait_ge(s_trf.h, 4 * (gc - 4))
                        for pi in range(2):
                            gh = bF + 2 * c + pi + 1
                            sc.wait_ge(s_fill.h, gh)
                            pp = psA if pi == 0 else psB
                            for qi, qt in enumerate((2 * pi, 2 * pi + 1)):
                                off = qi * 512
                                N = 512 if not diag else P * (qt + 1)
                                ins = sc.activation(
                                    p2scr[:, par * 2048 + qt * 512:
                                          par * 2048 + qt * 512 + N],
                                    pp[:, off:off + N], AF.Exp,
                                    bias=g.negC[:, 0:1],
                                    accum_out=la[:, qt * 8 + c:qt * 8 + c + 1])
                            ins.then_inc(s_mexp.h, 1)

            @b.vector
            def _(v):
                wdma(v, s_w2, 1)
                v.memset(ones_row[:], 1.0)

                def ladder_head(u):
                    v.wait_ge(s_bc.h, u + 1)
                    cp = v.tensor_copy(linvb[u % 2][:], psO[u % 2][:, 0:QB])
                    vch.sync(v, cp)
                    v.nop().then_inc(s_li.h, 1)
                    if u >= 2:
                        wdma(v, s_at, u - 1)
                    v.tensor_tensor(attn_h[u % 2][:], attn_raw[u % 2][:],
                                    linvb[u % 2][:], ALU.mult).then_inc(s_ae.h, 1)

                def lfin_block(hxp, nchp, hp):
                    lap = l_all2[:, hp * 32:(hp + 1) * 32]
                    v.wait_ge(s_mexp.h, sched[hxp]["bF"] + 2 * nchp)
                    if hxp >= 1:
                        v.wait_ge(s_lt.h, hxp)
                    for qt in range(4):
                        vins = v.reduce_sum(lfin[:, qt:qt + 1],
                                            lap[:, qt * 8:qt * 8 + nchp], axis=AX.X)
                    vch.sync(v, vins)
                    vch.sync(v, v.reciprocal(lfin[:], lfin[:]))
                    v.tensor_copy(lbf[hxp % 2][:], lfin[:]).then_inc(s_lf.h, 1)

                def raw_block(hxp, nchp):
                    v.wait_ge(s_pv.h, sched[hxp]["bT"] + 4 * nchp)
                    v.tensor_copy(attn_raw[hxp % 2][:],
                                  psO[hxp % 2][:]).then_inc(s_ar.h, 1)

                pend = []
                for u in sched:
                    qb, h, hx, nch = u["qb"], u["h"], u["hx"], u["nch"]
                    bF, bT, gc0 = u["bF"], u["bT"], u["gc0"]
                    if hx >= 2:
                        ladder_head(hx - 2)

                    def evacs(c):
                        diag = c == qb
                        gc = gc0 + c + 1
                        par = (gc - 1) % 3
                        for ktl in range(4):
                            gt = bT + 4 * c + ktl + 1
                            v.wait_ge(s_trf.h, gt)
                            if gc > 3:
                                v.wait_ge(s_pv.h, 4 * (gc - 3))
                            qoff = P * ktl if diag else 0
                            v.tensor_copy(
                                p2scrT[:, par * 2048 + ktl * 512 + qoff:
                                       par * 2048 + (ktl + 1) * 512],
                                psT[(gt - 1) % 2][:, qoff:512]).then_inc(s_tre.h, 1)

                    for c in range(nch):
                        evacs(c)
                        if pend:
                            pend.pop(0)()
                    while pend:
                        pend.pop(0)()
                    pend = [lambda hxp=hx, nchp=nch, hp=h: lfin_block(hxp, nchp, hp),
                            lambda hxp=hx, nchp=nch: raw_block(hxp, nchp)]
                    if hx >= 1:
                        v.wait_ge(s_lt.h, hx)
                        v.tensor_copy(lrow_bf[(hx - 1) % 2][:],
                                      psO[(hx - 1) % 2][0:1, 0:QB]).then_inc(s_lr.h, 1)
                while pend:
                    pend.pop(0)()
                ladder_head(NU - 2)
                v.wait_ge(s_lt.h, NU)
                v.tensor_copy(lrow_bf[(NU - 1) % 2][:],
                              psO[(NU - 1) % 2][0:1, 0:QB]).then_inc(s_lr.h, 1)
                ladder_head(NU - 1)
    return



def phase3(g, ctx):
    nc = g.nc
    begin_phase(g)
    with ExitStack() as px:
        pe_ = px.enter_context
        sb = lambda n, s, d: pe_(nc.sbuf_tensor(n, list(s), d))
        ps = lambda n, s: pe_(nc.psum_tensor(n, list(s), F32))
        at_sb = [sb(f"p3_at{i}", (P, D), BF16) for i in range(2)]
        xb3 = [sb(f"p3_xb{i}", (P, D), F32) for i in range(2)]
        ow_sb = sb("p3_ow", (P, DT * D), BF16)
        outb = sb("p3_outb", (P, D), F32)
        fnw = sb("p3_fnw", (P, DT), F32)
        gw_sb = sb("p3_gw", (P, DT * E), F32)
        xnf = sb("p3_xnf", (P, D), F32)
        h2f = sb("p3_h2f", (P, D), F32)
        h2bf = sb("p3_h2bf", (P, D), BF16)
        h2T = sb("p3_h2T", (P, D), BF16)
        restb = sb("p3_rest", (P, D), F32)
        h2r32 = sb("p3_h2r32", (P, D), F32)
        h2T32 = sb("p3_h2T32", (P, D), F32)
        t1 = sb("p3_t1", (P, 1024), F32)
        sqsc = sb("p3_sqsc", (P, 512), F32)
        hs2 = sb("p3_hs2", (P, 4), F32)
        rrc = sb("p3_rrc", (P, 1), F32)
        g3t = sb("p3_g3t", (P, 1), F32)
        gp0 = sb("p3_gp0", (P, E), F32)
        gnm = sb("p3_gnm", (P, 1), F32)
        gsum = sb("p3_gsum", (P, 1), F32)
        gm1 = sb("p3_gm1", (P, 1), F32)
        gm2 = sb("p3_gm2", (P, 1), F32)
        geq1 = sb("p3_geq1", (P, E), F32)
        gmask = sb("p3_gmask", (P, E), F32)
        gden = sb("p3_gden", (P, 1), F32)
        gw1 = sb("p3_gw1", (P, 1), F32)
        gw2 = sb("p3_gw2", (P, 1), F32)
        ps1 = [ps(f"p3_ps1{i}", (P, 1024)) for i in range(2)]
        ps2 = [ps(f"p3_ps2{i}", (P, 1024)) for i in range(2)]

        s_at3 = Sem(g, px, "p3_at")
        s_x3 = Sem(g, px, "p3_x")
        s_w3 = Sem(g, ctx, "p3_w", persist=True)
        s_of = Sem(g, px, "p3_of")
        s_xn = Sem(g, px, "p3_xn")
        s_h2 = Sem(g, px, "p3_h2")
        s_rd = Sem(g, px, "p3_rd")
        s_rs = Sem(g, px, "p3_rs")
        s_hs = Sem(g, px, "p3_hs")
        s_rr = Sem(g, px, "p3_rr")
        s_hb = Sem(g, px, "p3_hb")
        s_t2 = Sem(g, px, "p3_t2")
        s_e2 = Sem(g, px, "p3_e2")
        s_h2s = Sem(g, px, "p3_h2s")
        s_gf = Sem(g, px, "p3_gf")
        s_gx = Sem(g, px, "p3_gx")
        s_g1 = Sem(g, px, "p3_g1")
        s_gd = Sem(g, px, "p3_gd")
        s_gps = Sem(g, px, "p3_gps")
        s_t3 = Sem(g, px, "p3_t3")
        s_e3 = Sem(g, px, "p3_e3")
        vch = Chain(g, px, "p3_vch")
        sch = Chain(g, px, "p3_sch")

        emit_clears(g)
        with nc.Block("p3") as b:
            @b.sync
            def _(sp):
                for t in range(NT):
                    if t >= 2:
                        sp.wait_ge(s_of.h, 2 * t - 2)
                    sp.dma_start(
                        at_sb[t % 2][:].rearrange("p (j c) -> p j c", j=HQ),
                        g.attnT_d[:, :, t * P:(t + 1) * P].rearrange("j p c -> p j c"),
                    ).then_inc(s_at3.h, 16)
                    if t >= 2:
                        sp.wait_ge(s_rd.h, 2 * t - 2)
                    sp.dma_start(xb3[t % 2][:], g.x[t * P:(t + 1) * P, :]).then_inc(s_x3.h, 16)
                    sp.wait_ge(s_rd.h, 2 * t + 1)
                    sp.dma_start(g.rest_d[t * P:(t + 1) * P, 0:1024],
                                 restb[:, 0:1024]).then_inc(s_rs.h, 16)
                    sp.wait_ge(s_rd.h, 2 * t + 2)
                    sp.dma_start(g.rest_d[t * P:(t + 1) * P, 1024:2048],
                                 restb[:, 1024:2048]).then_inc(s_rs.h, 16)
                    sp.wait_ge(s_e2.h, 2 * t + 2)
                    if t >= 2:
                        wdma(sp, s_h2s, t - 1)
                    sp.dma_start(
                        g.h2T_d[:, :, t * P:(t + 1) * P].rearrange("k p c -> p k c"),
                        h2T[:].rearrange("p (k c) -> p k c", k=DT),
                    ).then_inc(s_h2s.h, 16)
                wdma(sp, s_rs, 2 * NT)
                wdma(sp, s_h2s, NT)
                if DBG:
                    sp.wait_ge(s_gd.h, NT)
                    sp.dma_start(g.dbg_cw.rearrange("t p e -> p t e"),
                                 g.cw_sb[:].rearrange("p (t e) -> p t e", t=NT)).then_inc(s_rs.h, 16)
                    wdma(sp, s_rs, 2 * NT + 1)

            @b.gpsimd
            def _(gp):
                ow3 = wload3(g.owT)
                osb3 = sb3(ow_sb[:], DT)
                for q4 in range(4):
                    gp.dma_start(osb3[:, q4 * 4:(q4 + 1) * 4, :],
                                 ow3[:, q4 * 4:(q4 + 1) * 4, :]).then_inc(s_w3.h, 16)
                gp.dma_start(outb[:], g.outb_r).then_inc(s_w3.h, 16)
                gp.dma_start(fnw[:], g.fnw_c).then_inc(s_w3.h, 16)
                gp.dma_start(sb3(gw_sb[:], DT), wload3(g.gwT)).then_inc(s_w3.h, 16)

            @b.tensor
            def _(t_):
                wdma(t_, s_w3, 7)
                for t in range(NT):
                    wdma(t_, s_at3, t + 1)
                    for hf in range(2):
                        pp = ps1[hf]
                        if t >= 1:
                            t_.wait_ge(s_h2.h, 2 * t - 1 + hf)
                        for jt in range(DT):
                            for ch in range(2):
                                ins = t_.matmul(
                                    pp[:, ch * 512:(ch + 1) * 512],
                                    at_sb[t % 2][:, jt * P:(jt + 1) * P],
                                    ow_sb[:, jt * D + hf * 1024 + ch * 512:
                                          jt * D + hf * 1024 + (ch + 1) * 512],
                                    start=(jt == 0), stop=(jt == DT - 1))
                        ins.then_inc(s_of.h, 1)
                    t_.wait_ge(s_hb.h, t + 1)
                    if t >= 1:
                        t_.wait_ge(s_e3.h, 2 * t)
                        t_.wait_ge(s_e2.h, 2 * t - 1)
                    transpose_blocks(t_, ps2[0], h2bf, g.ident_sb, 8,
                                     src_off=0).then_inc(s_t2.h, 1)
                    if t >= 1:
                        t_.wait_ge(s_e2.h, 2 * t)
                        t_.wait_ge(s_gps.h, t)
                    transpose_blocks(t_, ps2[1], h2bf, g.ident_sb, 8,
                                     src_off=8 * P).then_inc(s_t2.h, 1)
                    t_.wait_ge(s_e2.h, 2 * t + 2)
                    for half in range(2):
                        transpose_blocks(t_, ps2[half], h2r32, g.identf_sb, 8,
                                         src_off=half * 1024).then_inc(s_t3.h, 1)
                    t_.wait_ge(s_e3.h, 2 * t + 2)
                    for dt in range(DT):
                        ins = t_.matmul(ps2[0][:, 0:E],
                                        h2T32[:, dt * P:(dt + 1) * P],
                                        gw_sb[:, dt * E:(dt + 1) * E],
                                        start=(dt == 0), stop=(dt == DT - 1))
                    ins.then_inc(s_gf.h, 1)

            @b.scalar
            def _(sc):
                wdma(sc, s_w3, 7)
                for t in range(NT):
                    wdma(sc, s_x3, t + 1)
                    if t >= 1:
                        sc.wait_ge(s_rd.h, 2 * t)
                    sc.activation(xnf[:], xb3[t % 2][:], AF.Identity,
                                  scale=g.st_rstd[:, t:t + 1],
                                  bias=g.st_nmr[:, t:t + 1]).then_inc(s_xn.h, 1)
                    for hf in range(2):
                        sc.wait_ge(s_h2.h, 2 * t + hf + 1)
                        for cc in range(2):
                            hsl = slice(hf * 1024 + cc * 512, hf * 1024 + (cc + 1) * 512)
                            ins = sc.activation(sqsc[:], h2f[:, hsl], AF.Square,
                                                accum_out=hs2[:, 2 * hf + cc:2 * hf + cc + 1])
                        ins.then_inc(s_hs.h, 1)
                    sc.wait_ge(s_rr.h, t + 1)
                    sch.sync(sc, sc.activation(rrc[:], rrc[:], AF.Sqrt))
                    sch.sync(sc, sc.activation(h2r32[:], h2f[:], AF.Identity,
                                               scale=rrc[:]))
                    sc.activation(h2bf[:], h2r32[:], AF.Copy).then_inc(s_hb.h, 1)
                    for half in range(2):
                        sc.wait_ge(s_t2.h, 2 * t + half + 1)
                        if t >= 1 and half == 0:
                            wdma(sc, s_h2s, t)
                        for j in range(8):
                            dt = half * 8 + j
                            ins = sc.activation(h2T[:, dt * P:(dt + 1) * P],
                                                ps2[half][:, j * P:(j + 1) * P],
                                                AF.Copy, scale=fnw[:, dt:dt + 1])
                        ins.then_inc(s_e2.h, 1)
                    for half in range(2):
                        sc.wait_ge(s_t3.h, 2 * t + half + 1)
                        for j in range(8):
                            dt = half * 8 + j
                            ins = sc.activation(h2T32[:, dt * P:(dt + 1) * P],
                                                ps2[half][:, j * P:(j + 1) * P],
                                                AF.Copy, scale=fnw[:, dt:dt + 1])
                        ins.then_inc(s_e3.h, 1)
                    sc.wait_ge(s_g1.h, t + 1)
                    sc.activation(gp0[:], gp0[:], AF.Exp, bias=gnm[:],
                                  accum_out=gsum[:]).then_inc(s_gx.h, 1)

            @b.vector
            def _(v):
                for t in range(NT):
                    v.wait_ge(s_xn.h, t + 1)
                    for hf in range(2):
                        pp = ps1[hf]
                        hsl = slice(hf * 1024, (hf + 1) * 1024)
                        v.wait_ge(s_of.h, 2 * t + hf + 1)
                        v.tensor_tensor(h2f[:, hsl], pp[:], xnf[:, hsl],
                                        ALU.add).then_inc(s_h2.h, 1)
                        if t >= 1 and hf == 0:
                            wdma(v, s_rs, 2 * t)
                        v.tensor_tensor(t1[:], xb3[t % 2][:, hsl], xnf[:, hsl], ALU.add)
                        vch.sync(v, v.tensor_tensor(t1[:], t1[:], outb[:, hsl], ALU.add))
                        v.tensor_tensor(restb[:, hsl], t1[:], h2f[:, hsl],
                                        ALU.add).then_inc(s_rd.h, 1)
                    v.wait_ge(s_hs.h, 2 * t + 2)
                    v.tensor_tensor(rrc[:], hs2[:, 0:1], hs2[:, 1:2], ALU.add)
                    vch.sync(v, v.tensor_tensor(g3t[:], hs2[:, 2:3], hs2[:, 3:4], ALU.add))
                    vch.sync(v, v.tensor_tensor(rrc[:], rrc[:], g3t[:], ALU.add))
                    vch.sync(v, v.tensor_scalar(rrc[:], rrc[:], 1.0 / D, 1e-6,
                                                ALU.mult, ALU.add))
                    v.reciprocal(rrc[:], rrc[:]).then_inc(s_rr.h, 1)
                    v.wait_ge(s_gf.h, t + 1)
                    vch.sync(v, v.tensor_copy(gp0[:], ps2[0][:, 0:E]))
                    v.reduce_max(gnm[:], gp0[:], axis=AX.X, negate=True).then_inc(s_g1.h, 1)
                    v.wait_ge(s_gx.h, t + 1)
                    vch.sync(v, v.reciprocal(gsum[:], gsum[:]))
                    vch.sync(v, v.tensor_scalar_mul(gp0[:], gp0[:], gsum[:]))
                    v.nop().then_inc(s_gps.h, 1)
                    vch.sync(v, v.reduce_max(gm1[:], gp0[:], axis=AX.X))
                    vch.sync(v, v.tensor_scalar(geq1[:], gp0[:], gm1[:], None, ALU.is_equal))
                    vch.sync(v, v.tensor_tensor(gmask[:], gp0[:], geq1[:], ALU.mult))
                    vch.sync(v, v.tensor_tensor(gmask[:], gp0[:], gmask[:], ALU.subtract))
                    vch.sync(v, v.reduce_max(gm2[:], gmask[:], axis=AX.X))
                    vch.sync(v, v.tensor_tensor(gden[:], gm1[:], gm2[:], ALU.add))
                    vch.sync(v, v.tensor_scalar_add(gden[:], gden[:], 1e-9))
                    vch.sync(v, v.reciprocal(gden[:], gden[:]))
                    v.tensor_tensor(gw1[:], gm1[:], gden[:], ALU.mult)
                    vch.sync(v, v.tensor_tensor(gw2[:], gm2[:], gden[:], ALU.mult))
                    vch.sync(v, v.tensor_scalar(gmask[:], gmask[:], gm2[:], None, ALU.is_equal))
                    vch.sync(v, v.tensor_scalar_mul(geq1[:], geq1[:], gw1[:]))
                    v.scalar_tensor_tensor(g.cw_sb[:, t * E:(t + 1) * E], gmask[:],
                                           gw2[:], geq1[:], ALU.mult, ALU.add).then_inc(s_gd.h, 1)
    return


def phase4(g, ctx):
    nc = g.nc
    LH = L // 2
    TH = LH // P
    CH = 4
    begin_phase(g)
    with ExitStack() as px:
        pe_ = px.enter_context
        sb = lambda n, s, d: pe_(nc.sbuf_tensor(n, list(s), d))
        ps = lambda n, s: pe_(nc.psum_tensor(n, list(s), F32))
        xz = [sb(f"p4_xz{i}", (P, D), BF16) for i in range(2)]
        z1b = [sb(f"p4_z1b{i}", (P, D), BF16) for i in range(2)]
        z1T = sb("p4_z1T", (P, DT * LH), BF16)
        rmsw = sb("p4_rmsw", (P, DT), F32)
        Abc = sb("p4_Abc", (P, DT), F32)
        wA = [sb(f"p4_wA{i}", (P, D), BF16) for i in range(2)]
        wB = [sb(f"p4_wB{i}", (P, D), BF16) for i in range(2)]
        wD = [sb(f"p4_wD{i}", (P, D), BF16) for i in range(2)]
        wCa = [sb(f"p4_wCa{i}", (P, D), BF16) for i in range(2)]
        wCb = [sb(f"p4_wCb{i}", (P, D), BF16) for i in range(2)]
        a_sb = sb("p4_a", (P, LH), F32)
        u_sb = sb("p4_u", (P, LH), F32)
        h_sb = sb("p4_h", (P, LH), F32)
        c_sb = sb("p4_c", (P, LH), F32)
        dtmp = sb("p4_dtmp", (P, 512), F32)
        catmp = sb("p4_catmp", (P, LH), F32)
        ybf = [sb(f"p4_ybf{i}", (P, LH), BF16) for i in range(2)]
        carry = sb("p4_carry", (P, DT), F32)
        psp = [ps(f"p4_ps{i}", (P, 512)) for i in range(5)]

        s_xz = Sem(g, px, "p4_xz")
        s_w4 = Sem(g, ctx, "p4_w4", persist=True)
        s_wg = Sem(g, ctx, "p4_wg", persist=True)
        s_z1 = Sem(g, px, "p4_z1")
        s_tz = Sem(g, px, "p4_tz")
        s_ze = Sem(g, px, "p4_ze")
        s_pf = Sem(g, px, "p4_pf")
        s_ae = Sem(g, px, "p4_ae")
        s_ue = Sem(g, px, "p4_ue")
        s_ce = Sem(g, px, "p4_ce")
        s_cd = Sem(g, px, "p4_cd")
        s_sc = Sem(g, px, "p4_sc")
        s_ys = Sem(g, px, "p4_ys")
        vch = Chain(g, px, "p4_vch")
        sch = Chain(g, px, "p4_sch")

        emit_clears(g)
        with nc.Block("p4") as b:
            @b.sync
            def _(sp):
                for hf in range(2):
                    for tt in range(TH):
                        tau = hf * TH + tt
                        if tau >= 2:
                            sp.wait_ge(s_z1.h, tau - 1)
                        sp.dma_start(xz[tt % 2][:],
                                     g.xnbf_d[tau * P:(tau + 1) * P, :]).then_inc(s_xz.h, 16)
                    for gi in range(DT):
                        gg = hf * DT + gi
                        sp.wait_ge(s_sc.h, gg + 1)
                        if gg >= 2:
                            wdma(sp, s_ys, gg - 1)
                        sp.dma_start(g.yT_d[gi, :, hf * LH:(hf + 1) * LH],
                                     ybf[gg % 2][:]).then_inc(s_ys.h, 16)
                wdma(sp, s_ys, 2 * DT)

            @b.gpsimd
            def _(gp):
                gp.dma_start(rmsw[:], g.rmsw_c).then_inc(s_w4.h, 16)
                gp.dma_start(Abc[:], g.Ab_c).then_inc(s_w4.h, 16)
                for hf in range(2):
                    for gi in range(DT):
                        gg = hf * DT + gi
                        if gg >= 2:
                            gp.wait_ge(s_pf.h, 20 * (gg - 1))
                        cs = slice(gi * P, (gi + 1) * P)
                        for wsb, wdr in ((wA, g.AwT), (wB, g.BwT), (wD, g.DwT),
                                         (wCa, g.CawT), (wCb, g.CbwT)):
                            gp.dma_start(sb3(wsb[gg % 2][:], DT),
                                         wdr[:, cs].rearrange("(k p) n -> p k n", p=P)
                                         ).then_inc(s_wg.h, 16)

            @b.scalar
            def _(sc):
                wdma(sc, s_w4, 2)
                for hf in range(2):
                    if hf == 1:
                        sc.wait_ge(s_ue.h, 64)
                        sc.wait_ge(s_cd.h, 64)
                    for tt in range(TH):
                        tau = hf * TH + tt
                        wdma(sc, s_xz, tau + 1)
                        if tau >= 2:
                            sc.wait_ge(s_tz.h, 4 * tau - 4)
                        sc.activation(z1b[tt % 2][:], xz[tt % 2][:], AF.Identity,
                                      scale=g.st_s[:, tau:tau + 1]).then_inc(s_z1.h, 1)
                        for qtr in range(4):
                            sc.wait_ge(s_tz.h, 4 * tau + qtr + 1)
                            for j in range(4):
                                dt = qtr * 4 + j
                                ins = sc.activation(
                                    z1T[:, dt * LH + tt * P:dt * LH + (tt + 1) * P],
                                    psp[qtr][:, j * P:(j + 1) * P],
                                    AF.Copy, scale=rmsw[:, dt:dt + 1])
                            ins.then_inc(s_ze.h, 1)
                    for gi in range(DT):
                        gg = hf * DT + gi
                        for tc in range(CH):
                            cu = gg * CH + tc
                            if tc == 0 and gg >= 1:
                                sc.wait_ge(s_sc.h, gg)
                            sc.wait_ge(s_pf.h, 5 * cu + 1)
                            sc.activation(a_sb[:, tc * 512:(tc + 1) * 512],
                                          psp[0][:], AF.Sigmoid,
                                          bias=Abc[:, gi:gi + 1]).then_inc(s_ae.h, 1)
                            if cu >= 1:
                                sc.wait_ge(s_ue.h, cu)
                            sc.wait_ge(s_pf.h, 5 * cu + 3)
                            sc.activation(dtmp[:], psp[2][:],
                                          AF.Sigmoid).then_inc(s_ae.h, 1)
                            if cu >= 4:
                                sc.wait_ge(s_cd.h, cu - 3)
                            sc.wait_ge(s_pf.h, 5 * cu + 4)
                            sc.activation(catmp[:, tc * 512:(tc + 1) * 512],
                                          psp[3][:], AF.Silu).then_inc(s_ce.h, 1)

            @b.vector
            def _(v):
                for hf in range(2):
                    for gi in range(DT):
                        gg = hf * DT + gi
                        for tc in range(CH):
                            cu = gg * CH + tc
                            v.wait_ge(s_ae.h, 2 * cu + 2)
                            v.tensor_tensor(u_sb[:, tc * 512:(tc + 1) * 512],
                                            dtmp[:], psp[1][:],
                                            ALU.mult).then_inc(s_ue.h, 1)
                            v.wait_ge(s_ce.h, cu + 1)
                            v.wait_ge(s_pf.h, 5 * cu + 5)
                            v.tensor_tensor(c_sb[:, tc * 512:(tc + 1) * 512],
                                            catmp[:, tc * 512:(tc + 1) * 512],
                                            psp[4][:],
                                            ALU.mult).then_inc(s_cd.h, 1)
                        init = 0.0 if hf == 0 else carry[:, gi:gi + 1]
                        scn = v.tensor_tensor_scan(h_sb[:], a_sb[:], u_sb[:], init,
                                                   ALU.mult, ALU.add)
                        if hf == 0:
                            vch.sync(v, scn)
                            v.tensor_copy(carry[:, gi:gi + 1], h_sb[:, LH - 1:LH])
                        if gg >= 2:
                            wdma(v, s_ys, gg - 1)
                        v.tensor_tensor(ybf[gg % 2][:], c_sb[:], h_sb[:],
                                        ALU.mult).then_inc(s_sc.h, 1)

            @b.tensor
            def _(t_):
                for hf in range(2):
                    if hf == 1:
                        t_.wait_ge(s_ae.h, 2 * 64)
                        t_.wait_ge(s_ue.h, 64)
                        t_.wait_ge(s_ce.h, 64)
                        t_.wait_ge(s_cd.h, 64)
                    for tt in range(TH):
                        tau = hf * TH + tt
                        t_.wait_ge(s_z1.h, tau + 1)
                        for qtr in range(4):
                            if 4 * tau + qtr >= 4:
                                t_.wait_ge(s_ze.h, 4 * tau + qtr - 3)
                            transpose_blocks(t_, psp[qtr], z1b[tt % 2], g.ident_sb, 4,
                                             src_off=qtr * 4 * P).then_inc(s_tz.h, 1)
                    for gi in range(DT):
                        gg = hf * DT + gi
                        if gi == 0:
                            t_.wait_ge(s_ze.h, 64 * (hf + 1))
                        wdma(t_, s_wg, 5 * (gg + 1))
                        for tc in range(CH):
                            cu = gg * CH + tc
                            for pj, wsb in ((0, wA), (1, wB), (2, wD)):
                                if cu >= 1:
                                    if pj == 0:
                                        t_.wait_ge(s_ae.h, 2 * cu - 1)
                                    elif pj == 1:
                                        t_.wait_ge(s_ue.h, cu)
                                    else:
                                        t_.wait_ge(s_ae.h, 2 * cu)
                                for dt in range(DT):
                                    ins = t_.matmul(
                                        psp[pj][:],
                                        wsb[gg % 2][:, dt * P:(dt + 1) * P],
                                        z1T[:, dt * LH + tc * 512:dt * LH + (tc + 1) * 512],
                                        start=(dt == 0), stop=(dt == DT - 1))
                                ins.then_inc(s_pf.h, 1)
                            if cu >= 1:
                                t_.wait_ge(s_ce.h, cu)
                                t_.wait_ge(s_cd.h, cu)
                            for pj, wsb in ((3, wCa), (4, wCb)):
                                for dt in range(DT):
                                    ins = t_.matmul(
                                        psp[pj][:],
                                        wsb[gg % 2][:, dt * P:(dt + 1) * P],
                                        z1T[:, dt * LH + tc * 512:dt * LH + (tc + 1) * 512],
                                        start=(dt == 0), stop=(dt == DT - 1))
                                ins.then_inc(s_pf.h, 1)
    return


def phase5(g, ctx):
    nc = g.nc
    LH = L // 2
    begin_phase(g)
    with ExitStack() as px:
        pe_ = px.enter_context
        sb = lambda n, s, d: pe_(nc.sbuf_tensor(n, list(s), d))
        ps = lambda n, s: pe_(nc.psum_tensor(n, list(s), F32))
        ow5 = sb("p5_ow", (P, DT * D), BF16)
        yt_sb = sb("p5_yt", (P, DT * LH), BF16)
        rb = [sb(f"p5_rb{i}", (P, D), F32) for i in range(2)]
        r2b = [sb(f"p5_r2b{i}", (P, D), F32) for i in range(2)]
        ps5 = [ps(f"p5_ps{i}", (P, D)) for i in range(2)]

        s_w5 = Sem(g, ctx, "p5_w", persist=True)
        s_yt = Sem(g, px, "p5_yt")
        s_r1 = Sem(g, px, "p5_r1")
        s_f5 = Sem(g, px, "p5_f")
        s_d5 = Sem(g, px, "p5_d")
        s_rs2 = Sem(g, px, "p5_rs2")

        emit_clears(g)
        with nc.Block("p5") as b:
            @b.sync
            def _(sp):
                for hf in range(2):
                    if hf == 1:
                        sp.wait_ge(s_f5.h, 16)
                    for q4 in range(4):
                        sp.dma_start(
                            yt_sb[:].rearrange("p (k t) -> p k t", k=DT)[:, q4 * 4:(q4 + 1) * 4, :],
                            g.yT_d[q4 * 4:(q4 + 1) * 4, :, hf * LH:(hf + 1) * LH].rearrange("k p t -> p k t"),
                        ).then_inc(s_yt.h, 16)
                    for tt in range(16):
                        t = hf * 16 + tt
                        if t >= 2:
                            sp.wait_ge(s_d5.h, t - 1)
                        sp.dma_start(rb[t % 2][:],
                                     g.rest_d[t * P:(t + 1) * P, :]).then_inc(s_r1.h, 16)
                        sp.wait_ge(s_d5.h, t + 1)
                        if t >= 2:
                            wdma(sp, s_rs2, t - 1)
                        sp.dma_start(g.rest2_d[t * P:(t + 1) * P, :],
                                     r2b[t % 2][:]).then_inc(s_rs2.h, 16)
                wdma(sp, s_rs2, NT)

            @b.gpsimd
            def _(gp):
                ow3 = wload3(g.outwT)
                osb3 = sb3(ow5[:], DT)
                for q4 in range(4):
                    gp.dma_start(osb3[:, q4 * 4:(q4 + 1) * 4, :],
                                 ow3[:, q4 * 4:(q4 + 1) * 4, :]).then_inc(s_w5.h, 16)

            @b.tensor
            def _(t_):
                wdma(t_, s_w5, 4)
                for hf in range(2):
                    wdma(t_, s_yt, 4 * (hf + 1))
                    for tt in range(16):
                        t = hf * 16 + tt
                        pp = ps5[t % 2]
                        if t >= 2:
                            t_.wait_ge(s_d5.h, t - 1)
                        for gi in range(DT):
                            for ch in range(4):
                                ins = t_.matmul(
                                    pp[:, ch * 512:(ch + 1) * 512],
                                    yt_sb[:, gi * LH + tt * P:gi * LH + (tt + 1) * P],
                                    ow5[:, gi * D + ch * 512:gi * D + (ch + 1) * 512],
                                    start=(gi == 0), stop=(gi == DT - 1))
                        ins.then_inc(s_f5.h, 1)

            @b.vector
            def _(v):
                for t in range(NT):
                    v.wait_ge(s_f5.h, t + 1)
                    wdma(v, s_r1, t + 1)
                    if t >= 2:
                        wdma(v, s_rs2, t - 1)
                    v.tensor_tensor(r2b[t % 2][:], ps5[t % 2][:], rb[t % 2][:],
                                    ALU.add).then_inc(s_d5.h, 1)
    return


def phase6(g, ctx):
    nc = g.nc
    TQ = 8
    NEQ = NQ * E
    begin_phase(g)
    with ExitStack() as px:
        pe_ = px.enter_context
        sb = lambda n, s, d: pe_(nc.sbuf_tensor(n, list(s), d))
        ps = lambda n, s: pe_(nc.psum_tensor(n, list(s), F32))
        h2q = sb("p6_h2q", (P, DT * QL), BF16)
        wg6 = [sb(f"p6_wg{i}", (P, D), BF16) for i in range(2)]
        wu6 = [sb(f"p6_wu{i}", (P, D), BF16) for i in range(2)]
        wd6 = sb("p6_wd", (P, IT * D), BF16)
        sg = sb("p6_sg", (P, QL), BF16)
        act6 = [sb(f"p6_act{i}", (P, IT * QL), BF16) for i in range(2)]
        acc = sb("p6_acc", (P, TQ * D), F32)
        psG = ps("p6_psG", (P, QL))
        psU = ps("p6_psU", (P, QL))
        psD = [ps(f"p6_psD{i}", (P, QL)) for i in range(2)]

        s_h2q = Sem(g, px, "p6_h2q")
        s_acc = Sem(g, px, "p6_acc")
        s_wg6 = Sem(g, ctx, "p6_wg6", persist=True)
        s_wu6 = Sem(g, ctx, "p6_wu6", persist=True)
        s_wd = Sem(g, ctx, "p6_wd", persist=True)
        s_gf = Sem(g, px, "p6_gf")
        s_uf = Sem(g, px, "p6_uf")
        s_sil = Sem(g, px, "p6_sil")
        s_a6 = Sem(g, px, "p6_a6")
        s_eof = Sem(g, px, "p6_eof")
        s_ev = Sem(g, px, "p6_ev")
        s_ys6 = Sem(g, px, "p6_ys")

        def oidx(peq, tb, dh):
            return 16 * peq + 2 * tb + dh + 1

        emit_clears(g)
        with nc.Block("p6") as b:
            @b.sync
            def _(sp):
                for Q in range(NQ):
                    if Q >= 1:
                        sp.wait_ge(s_uf.h, E * IT * Q)
                    h3 = h2q[:].rearrange("p (k t) -> p k t", k=DT)
                    for q4 in range(4):
                        sp.dma_start(
                            h3[:, q4 * 4:(q4 + 1) * 4, :],
                            g.h2T_d[q4 * 4:(q4 + 1) * 4, :, Q * QL:(Q + 1) * QL]
                            .rearrange("k p t -> p k t"),
                        ).then_inc(s_h2q.h, 16)
                    if Q >= 1:
                        sp.wait_ge(s_ev.h, 16 * E * Q)
                        sp.dma_start(
                            g.y[(Q - 1) * QL:Q * QL, :].rearrange("(t p) d -> p t d", p=P),
                            acc[:].rearrange("p (t d) -> p t d", t=TQ),
                        ).then_inc(s_ys6.h, 16)
                    sp.dma_start(
                        acc[:].rearrange("p (t d) -> p t d", t=TQ),
                        g.rest2_d[Q * QL:(Q + 1) * QL, :]
                        .rearrange("(t p) d -> p t d", p=P),
                    ).then_inc(s_acc.h, 16)
                    for e in range(E):
                        eq = Q * E + e
                        if eq == 0:
                            wdma(sp, g.s_pcw, E * IT)
                        else:
                            sp.wait_ge(s_eof.h, 16 * eq)
                        sp.dma_start(
                            sb3(wd6[:], IT)[:, 6:IT, :],
                            g.wdbf_d[e, 6:IT, :, :].rearrange("k p n -> p k n"),
                        ).then_inc(s_wd.h, 16)
                sp.wait_ge(s_ev.h, 16 * E * NQ)
                sp.dma_start(
                    g.y[(NQ - 1) * QL:NQ * QL, :].rearrange("(t p) d -> p t d", p=P),
                    acc[:].rearrange("p (t d) -> p t d", t=TQ),
                ).then_inc(s_ys6.h, 16)
                wdma(sp, s_ys6, NQ)

            @b.gpsimd
            def _(gp):
                for Q in range(NQ):
                    for e in range(E):
                        for it in range(IT):
                            git = (Q * E + e) * IT + it + 1
                            if git >= 3:
                                gp.wait_ge(s_gf.h, git - 2)
                            gp.dma_start(wg6[(git - 1) % 2][:]
                                         .rearrange("p (k n) -> p k n", k=DT),
                                         g.wgT[e, :, it * P:(it + 1) * P]
                                         .rearrange("(k p) n -> p k n", p=P)
                                         ).then_inc(s_wg6.h, 16)
                            if git >= 3:
                                gp.wait_ge(s_uf.h, git - 2)
                            gp.dma_start(wu6[(git - 1) % 2][:]
                                         .rearrange("p (k n) -> p k n", k=DT),
                                         g.wuT[e, :, it * P:(it + 1) * P]
                                         .rearrange("(k p) n -> p k n", p=P)
                                         ).then_inc(s_wu6.h, 16)

            @b.tensor
            def _(t_):
                def eo(peq, tb, dh):
                    pd = psD[dh]
                    if tb == 0 and dh == 0:
                        t_.wait_ge(s_a6.h, IT * (peq + 1))
                        wdma(t_, s_wd, 2 * (peq + 1))
                    oi = oidx(peq, tb, dh)
                    if oi > 2:
                        t_.wait_ge(s_ev.h, oi - 2)
                    ins = None
                    for it in range(IT):
                        for c2 in range(2):
                            ins = t_.matmul(
                                pd[:, c2 * 512:(c2 + 1) * 512],
                                act6[peq % 2][:, it * QL + tb * P:it * QL + (tb + 1) * P],
                                wd6[:, it * D + dh * 1024 + c2 * 512:
                                    it * D + dh * 1024 + (c2 + 1) * 512],
                                start=(it == 0), stop=(it == IT - 1))
                    ins.then_inc(s_eof.h, 1)

                for Q in range(NQ):
                    wdma(t_, s_h2q, 4 * (Q + 1))
                    for e in range(E):
                        eq = Q * E + e
                        for it in range(IT):
                            git = eq * IT + it + 1
                            wdma(t_, s_wg6, git)
                            if git >= 2:
                                t_.wait_ge(s_sil.h, git - 1)
                            ins = None
                            for dt in range(DT):
                                for ch in range(2):
                                    ins = t_.matmul(
                                        psG[:, ch * 512:(ch + 1) * 512],
                                        wg6[(git - 1) % 2][:, dt * P:(dt + 1) * P],
                                        h2q[:, dt * QL + ch * 512:dt * QL + (ch + 1) * 512],
                                        start=(dt == 0), stop=(dt == DT - 1))
                            ins.then_inc(s_gf.h, 1)
                            wdma(t_, s_wu6, git)
                            if git >= 2:
                                t_.wait_ge(s_a6.h, git - 1)
                            ins = None
                            for dt in range(DT):
                                for ch in range(2):
                                    ins = t_.matmul(
                                        psU[:, ch * 512:(ch + 1) * 512],
                                        wu6[(git - 1) % 2][:, dt * P:(dt + 1) * P],
                                        h2q[:, dt * QL + ch * 512:dt * QL + (ch + 1) * 512],
                                        start=(dt == 0), stop=(dt == DT - 1))
                            ins.then_inc(s_uf.h, 1)
                            if eq >= 1 and it < TQ:
                                eo(eq - 1, it, 0)
                                eo(eq - 1, it, 1)
                for tb in range(TQ):
                    eo(NEQ - 1, tb, 0)
                    eo(NEQ - 1, tb, 1)

            @b.scalar
            def _(sc):
                for Q in range(NQ):
                    for e in range(E):
                        eq = Q * E + e
                        for it in range(IT):
                            git = eq * IT + it + 1
                            if it == TQ:
                                if eq == 0:
                                    wdma(sc, g.s_pcw, E * IT)
                                else:
                                    sc.wait_ge(s_eof.h, 16 * eq)
                                sc.dma_start(
                                    sb3(wd6[:], IT)[:, 0:6, :],
                                    g.wdbf_d[e, 0:6, :, :].rearrange("k p n -> p k n"),
                                ).then_inc(s_wd.h, 16)
                            sc.wait_ge(s_gf.h, git)
                            if git >= 2:
                                sc.wait_ge(s_a6.h, git - 1)
                            sc.activation(sg[:], psG[:], AF.Silu).then_inc(s_sil.h, 1)

            @b.vector
            def _(v):
                def evac(peq, tb):
                    Qp, ep = divmod(peq, E)
                    if ep == 0 and tb == 0:
                        wdma(v, s_acc, Qp + 1)
                    for dh in range(2):
                        oi = oidx(peq, tb, dh)
                        v.wait_ge(s_eof.h, oi)
                        cwc = g.cw_sb[:, (Qp * TQ + tb) * E + ep:
                                      (Qp * TQ + tb) * E + ep + 1]
                        v.scalar_tensor_tensor(
                            acc[:, tb * D + dh * 1024:tb * D + (dh + 1) * 1024],
                            psD[dh][:], cwc,
                            acc[:, tb * D + dh * 1024:tb * D + (dh + 1) * 1024],
                            ALU.mult, ALU.add).then_inc(s_ev.h, 1)

                for Q in range(NQ):
                    for e in range(E):
                        eq = Q * E + e
                        for it in range(IT):
                            git = eq * IT + it + 1
                            v.wait_ge(s_uf.h, git)
                            v.wait_ge(s_sil.h, git)
                            if it == 0 and eq >= 2:
                                v.wait_ge(s_eof.h, 16 * (eq - 1))
                            v.tensor_tensor(act6[eq % 2][:, it * QL:(it + 1) * QL],
                                            sg[:], psU[:], ALU.mult).then_inc(s_a6.h, 1)
                            if eq >= 1 and it < TQ:
                                evac(eq - 1, it)
                for tb in range(TQ):
                    evac(NEQ - 1, tb)
    return



def _get_built():
    if "g" not in _CACHE:
        _CACHE["g"] = build()
    return _CACHE["g"]


def kernel(**inputs):
    g = _get_built()
    sh = _CACHE.get("sh")
    if sh is None or _CACHE.get("sh_id") != id(inputs["L0_attn_A_w"]):
        sh = _prep_shared(inputs)
        _CACHE["sh"] = sh
        _CACHE["sh_id"] = id(inputs["L0_attn_A_w"])
    x = inputs["L0_attn_x"]
    in_maps = []
    for c in range(8):
        m = dict(sh)
        m["x"] = np.ascontiguousarray(x[c])
        in_maps.append(m)
    res = run_bass_kernel_spmd(g.nc, in_maps, core_ids=list(range(8)))
    out = np.stack([np.asarray(r["y"]) for r in res.results], axis=0)
    return out


# version 14 (final)
# speedup vs baseline: 1.0381x; 1.0381x over previous
import math
import os
from contextlib import ExitStack

import numpy as np
import ml_dtypes

import concourse.bass as bass
import concourse.mybir as mybir
from concourse.bass_utils import run_bass_kernel_spmd

F32 = mybir.dt.float32
BF16 = mybir.dt.bfloat16
AF = mybir.ActivationFunctionType
ALU = mybir.AluOpType
AX = mybir.AxisListType

P = 128
L = 4096
D = 2048
NT = L // P
DT = D // P
HQ, HKV, HD = 16, 4, 128
NKV = HKV * HD
E, I = 8, 1408
IT = I // P
SCALE = 1.0 / math.sqrt(HD)
QB = 512
NQB = L // QB
NQ = 4
QL = L // NQ
ROPE_THETA = 1e6
NEG = -1.0e30

DBG = bool(int(os.environ.get("MK_DBG", "0")))
NOPC = bool(int(os.environ.get("MK_NOPC", "0")))
P0LVL = int(os.environ.get("MK_P0LVL", "9"))

_CACHE = {}



def _host_consts():
    freqs = 1.0 / ROPE_THETA ** (np.arange(0, HD, 2, dtype=np.float32) / HD)
    t = np.arange(L, dtype=np.float32)
    f = np.outer(t, freqs)
    cos = np.cos(f).astype(np.float32)
    sin = np.sin(f).astype(np.float32)
    cs2 = np.concatenate([cos, cos], axis=1)
    sn2 = np.concatenate([sin, sin], axis=1)
    out = dict(
        cosq=np.ascontiguousarray(np.tile(cs2, (1, HQ))),
        sinq=np.ascontiguousarray(np.tile(sn2, (1, HQ))),
        cosk=np.ascontiguousarray(np.tile(cs2, (1, HKV))),
        sink=np.ascontiguousarray(np.tile(sn2, (1, HKV))),
        ident=np.eye(P, dtype=np.float32).astype(ml_dtypes.bfloat16),
        identf=np.eye(P, dtype=np.float32),
    )
    ii = np.arange(P)
    out["mask_lt"] = np.where(ii[None, :] <= ii[:, None], 0.0, NEG).astype(np.float32)
    out["mask_ut"] = np.where(ii[:, None] <= ii[None, :], 0.0, NEG).astype(np.float32)
    out["mask_bf"] = out["mask_lt"].astype(ml_dtypes.bfloat16)
    return out


def _prep_shared(inputs):
    c = np.ascontiguousarray
    sh = {}
    sh["AwT"] = c(inputs["L0_attn_A_w"].T)
    sh["BwT"] = c(inputs["L0_attn_B_w"].T)
    sh["CawT"] = c(inputs["L0_attn_Ca_w"].T)
    sh["CbwT"] = c(inputs["L0_attn_Cb_w"].T)
    sh["DwT"] = c(inputs["L0_attn_D_w"].T)
    sh["outwT"] = c(inputs["L0_attn_out_w"].T)
    sh["qwT"] = c(inputs["L0_ffn_q_w"].T)
    sh["kwT"] = c(inputs["L0_ffn_k_w"].T)
    sh["vwT"] = c(inputs["L0_ffn_v_w"].T)
    sh["owT"] = c(inputs["L0_ffn_o_w"].T)
    sh["gwT"] = c(inputs["L0_ffn_gate_w"].T)
    sh["wgT"] = c(inputs["L0_ffn_wg"].transpose(0, 2, 1))
    sh["wuT"] = c(inputs["L0_ffn_wu"].transpose(0, 2, 1))
    sh["wdT"] = c(inputs["L0_ffn_wd"].transpose(0, 2, 1))
    cols = lambda v: c(v.reshape(DT, P).T)
    sh["anw_c"] = cols(inputs["L0_ffn_attn_norm_w"])
    sh["fnw_c"] = cols(inputs["L0_ffn_ffn_norm_w"])
    sh["rmsw_c"] = cols(inputs["L0_attn_rms_w"])
    sh["Ab_c"] = cols(inputs["L0_attn_A_b"])
    sh["outb_r"] = c(np.broadcast_to(inputs["L0_attn_out_b"], (P, D)))
    sh["qnw_r"] = c(np.broadcast_to(np.tile(inputs["L0_ffn_q_norm_w"], HQ), (P, D)))
    sh["knw_r"] = c(np.broadcast_to(np.tile(inputs["L0_ffn_k_norm_w"], HKV), (P, NKV)))
    sh.update(_host_consts())
    return sh



class Ctx:
    pass


def declare_io(nc):
    g = Ctx()
    g.nc = nc
    dr = lambda n, s, d: nc.dram_tensor(n, list(s), d, kind="ExternalInput").ap()
    g.x = dr("x", (L, D), F32)
    for n, s in [("AwT", (D, D)), ("BwT", (D, D)), ("CawT", (D, D)),
                 ("CbwT", (D, D)), ("DwT", (D, D)), ("outwT", (D, D)),
                 ("qwT", (D, D)), ("kwT", (D, NKV)), ("vwT", (D, NKV)),
                 ("owT", (D, D)), ("gwT", (D, E)),
                 ("wgT", (E, D, I)), ("wuT", (E, D, I)), ("wdT", (E, I, D)),
                 ("anw_c", (P, DT)), ("fnw_c", (P, DT)), ("rmsw_c", (P, DT)),
                 ("Ab_c", (P, DT)), ("outb_r", (P, D)), ("qnw_r", (P, D)),
                 ("knw_r", (P, NKV)), ("cosq", (L, D)), ("sinq", (L, D)),
                 ("cosk", (L, NKV)), ("sink", (L, NKV)),
                 ("mask_lt", (P, P)), ("mask_ut", (P, P))]:
        setattr(g, n, dr(n, s, F32))
    g.ident = dr("ident", (P, P), BF16)
    g.mask_bf = nc.dram_tensor("mask_bf", [P, P], BF16, kind="ExternalInput").ap()
    g.identf = nc.dram_tensor("identf", [P, P], F32, kind="ExternalInput").ap()

    g.y = nc.dram_tensor("y", [L, D], F32, kind="ExternalOutput").ap()

    scr_kind = "ExternalOutput" if DBG else "Internal"
    scr = lambda n, s, d: nc.dram_tensor(n, list(s), d, kind=scr_kind).ap()
    g.xnbf_d = scr("xnbf_d", (L, D), BF16)
    g.attnT_d = scr("attnT_d", (HQ, P, L), BF16)
    g.qT_d = scr("qT_d", (HQ, P, L), BF16)
    g.h2T_d = scr("h2T_d", (DT, P, L), BF16)
    g.yT_d = scr("yT_d", (DT, P, L), BF16)
    g.rest_d = scr("rest_d", (L, D), F32)
    g.rest2_d = scr("rest2_d", (L, D), F32)
    g.wdbf_d = nc.dram_tensor("wdbf_d", [E, IT, P, D], BF16, kind="Internal").ap()
    if DBG:
        g.dbg_kT = scr("dbg_kT", (HKV, P, L), BF16)
        g.dbg_v = scr("dbg_v", (NT, P, NKV), BF16)
        g.dbg_qT = scr("dbg_qT", (HQ, P, L), BF16)
        g.dbg_cw = scr("dbg_cw", (NT, P, E), F32)
        g.dbg_st = scr("dbg_st", (4, P, NT), F32)
    return g


class Sem:
    def __init__(self, g, ctx, name, persist=False):
        self.h = ctx.enter_context(g.nc.semaphore(name))
        if getattr(g, "_clr", None) is not None:
            g._clr.append(self.h)


def begin_phase(g):
    g._clr = []


_CLR_N = [0]


def emit_clears(g):
    nc = g.nc
    sems = g._clr
    g._clr = None
    _CLR_N[0] += 1
    with nc.Block(f"clr{_CLR_N[0]}") as cb:
        @cb.sync
        def _(sp):
            for h in sems:
                sp.sem_clear(h)


class Chain:

    def __init__(self, g, ctx, name):
        self.h = ctx.enter_context(g.nc.semaphore(name))
        self.n = 0
        if getattr(g, "_clr", None) is not None:
            g._clr.append(self.h)

    def sync(self, eng, ins):
        self.n += 1
        ins.then_inc(self.h, 1)
        eng.wait_ge(self.h, self.n)


def wdma(eng, sem, n):
    eng.wait_ge(sem.h, 16 * n)


def transpose_blocks(t, psdst, src_bf, ident, nblk, src_off=0, dst_off=0,
                     bank_cols=512):
    ins = None
    for j in range(nblk):
        o = dst_off + j * P
        ins = t.matmul(psdst[:, o:o + P],
                       src_bf[:, src_off + j * P: src_off + (j + 1) * P],
                       ident[:, :],
                       start=(o % bank_cols == 0),
                       stop=(o % bank_cols == bank_cols - P or j == nblk - 1))
    return ins



def wload3(dram2d):
    return dram2d.rearrange("(k p) n -> p k n", p=P)


def sb3(sb_ap, k):
    return sb_ap.rearrange("p (k n) -> p k n", k=k)



def phase0(g, ctx, actx):
    nc = g.nc
    begin_phase(g)
    with ExitStack() as px:
        pe_ = px.enter_context
        sb = lambda n, s, d: pe_(nc.sbuf_tensor(n, list(s), d))
        ps = lambda n, s: pe_(nc.psum_tensor(n, list(s), F32))
        xb = [sb(f"p0_xb{i}", (P, D), F32) for i in range(4)]
        ck = [sb(f"p0_ck{i}", (P, NKV), F32) for i in range(3)]
        sk = [sb(f"p0_sk{i}", (P, NKV), F32) for i in range(3)]
        xnbf = [sb(f"p0_xnbf{i}", (P, D), BF16) for i in range(2)]
        z2bf = [sb(f"p0_z2bf{i}", (P, D), BF16) for i in range(2)]
        xnchk = sb("p0_xnchk", (P, 1), BF16)
        z2T = sb("p0_z2T", (P, D), BF16)
        kw_sb = sb("p0_kw", (P, DT * NKV), BF16)
        vw_sb = sb("p0_vw", (P, DT * NKV), BF16)
        anw = sb("p0_anw", (P, DT), F32)
        knwr = sb("p0_knwr", (P, NKV), F32)
        sq = sb("p0_sq", (P, D), F32)
        kf = sb("p0_kf", (P, NKV), F32)
        kf0 = sb("p0_kf0", (P, NKV), F32)
        krot = sb("p0_krot", (P, NKV), F32)
        kt1 = sb("p0_kt1", (P, NKV), F32)
        krope = sb("p0_krope", (P, NKV), BF16)
        w = [sb(f"p0_w{i}", (P, 1), F32) for i in range(9)]
        kms = sb("p0_kms", (P, HKV), F32)
        kms2t = sb("p0_kms2t", (P, HKV), F32)
        krms = sb("p0_krms", (P, HKV), F32)
        zsc = sb("p0_zsc", (P, 1), F32)
        zbi = sb("p0_zbi", (P, 1), F32)
        psZa = ps("p0_psZa", (P, 1024))
        psZb = ps("p0_psZb", (P, 1024))
        psk = [ps(f"p0_psk{i}", (P, NKV)) for i in range(2)]
        psv = [ps(f"p0_psv{i}", (P, NKV)) for i in range(2)]

        s_x = Sem(g, px, "p0_sx")
        s_cs = Sem(g, px, "p0_scs")
        s_w = Sem(g, ctx, "p0_sw", persist=True)
        s_vst = Sem(g, px, "p0_vst")
        s_sqr = Sem(g, px, "p0_sqr")
        s_act = Sem(g, px, "p0_act")
        s_zv = Sem(g, px, "p0_zv")
        s_wsc = Sem(g, px, "p0_wsc")
        s_zf = Sem(g, px, "p0_zf")
        s_zd = Sem(g, px, "p0_zd")
        s_kvf = Sem(g, px, "p0_kvf")
        s_sq1 = Sem(g, px, "p0_sq1")
        s_kms = Sem(g, px, "p0_kms_s")
        s_kn1 = Sem(g, px, "p0_kn1")
        s_kn2 = Sem(g, px, "p0_kn2")
        s_kn = Sem(g, px, "p0_kn")
        s_kro = Sem(g, px, "p0_kro")
        s_ktf = Sem(g, px, "p0_ktf")
        s_ktd = Sem(g, px, "p0_ktd")
        s_vd = Sem(g, px, "p0_vd")
        s_kfr = Sem(g, px, "p0_kfr")
        s_ksq = Sem(g, px, "p0_ksq")
        s_kmx = Sem(g, px, "p0_kmx")
        vch = Chain(g, px, "p0_vch")
        sch = Chain(g, px, "p0_sch")

        emit_clears(g)
        with nc.Block("p0") as b:
            @b.sync
            def _(sp):
                for t in range(NT):
                    if t >= 4:
                        sp.wait_ge(s_act.h, t - 3)
                        sp.wait_ge(s_zv.h, t - 3)
                    sp.dma_start(xb[t % 4][:], g.x[t * P:(t + 1) * P, :]).then_inc(s_x.h, 16)
                    if t >= 3 and P0LVL >= 4:
                        sp.wait_ge(s_kro.h, t - 2)
                    sp.dma_start(ck[t % 3][:], g.cosk[t * P:(t + 1) * P, :]).then_inc(s_cs.h, 16)
                    sp.dma_start(sk[t % 3][:], g.sink[t * P:(t + 1) * P, :]).then_inc(s_cs.h, 16)
                wdma(sp, g.s_xnb, NT)
                if DBG and P0LVL >= 4:
                    sp.wait_ge(s_ktd.h, NT)
                    sp.wait_ge(s_vd.h, NT)
                    sp.dma_start(g.dbg_kT.rearrange("h p l -> p h l"), g.kT_sb[:].rearrange("p (h l) -> p h l", h=HKV)).then_inc(s_x.h, 16)
                    sp.dma_start(g.dbg_v.rearrange("t p n -> p t n"), g.v_sb[:].rearrange("p (t n) -> p t n", t=NT)).then_inc(s_x.h, 16)
                    sp.dma_start(g.dbg_st[0], g.st_rstd[:]).then_inc(s_x.h, 16)
                    sp.dma_start(g.dbg_st[1], g.st_nmr[:]).then_inc(s_x.h, 16)
                    sp.dma_start(g.dbg_st[2], g.st_s[:]).then_inc(s_x.h, 16)
                    sp.dma_start(g.dbg_st[3], g.st_r[:]).then_inc(s_x.h, 16)
                    sp.wait_ge(s_x.h, 16 * (NT + 6))

            @b.gpsimd
            def _(gp):
                gp.dma_start(sb3(kw_sb[:], DT), wload3(g.kwT)).then_inc(s_w.h, 16)
                gp.dma_start(sb3(vw_sb[:], DT), wload3(g.vwT)).then_inc(s_w.h, 16)
                gp.dma_start(g.ident_sb[:], g.ident).then_inc(s_w.h, 16)
                gp.dma_start(g.identf_sb[:], g.identf).then_inc(s_w.h, 16)
                gp.dma_start(anw[:], g.anw_c).then_inc(s_w.h, 16)
                gp.dma_start(knwr[:], g.knw_r).then_inc(s_w.h, 16)

            @b.vector
            def _(v):
                wdma(v, s_w, 6)
                v.memset(g.kmax2[:], 0.0)
                v.tensor_scalar_mul(knwr[:], knwr[:], SCALE ** 0.5)
                for dt in range(DT):
                    v.tensor_scalar_mul(kw_sb[:, dt * NKV:(dt + 1) * NKV],
                                        kw_sb[:, dt * NKV:(dt + 1) * NKV],
                                        anw[:, dt:dt + 1])
                    vins = v.tensor_scalar_mul(vw_sb[:, dt * NKV:(dt + 1) * NKV],
                                               vw_sb[:, dt * NKV:(dt + 1) * NKV],
                                               anw[:, dt:dt + 1])
                vch.sync(v, vins)
                v.nop().then_inc(s_wsc.h, 1)
                for t in range(NT):
                    wdma(v, s_x, t + 1)
                    x_ = xb[t % 4]
                    vch.sync(v, v.reduce_sum(w[0][:], x_[:], axis=AX.X))
                    v.wait_ge(s_sq1.h, t + 1)
                    vch.sync(v, v.tensor_scalar_mul(w[2][:], w[0][:], 1.0 / D))
                    vch.sync(v, v.tensor_tensor(w[3][:], w[2][:], w[2][:], ALU.mult))
                    vch.sync(v, v.scalar_tensor_tensor(w[4][:], w[1][:], 1.0 / D, w[3][:],
                                                       ALU.mult, ALU.subtract))
                    vch.sync(v, v.tensor_scalar_add(w[5][:], w[4][:], 1e-5))
                    vch.sync(v, v.reciprocal(w[5][:], w[5][:]))
                    vch.sync(v, v.tensor_tensor(w[6][:], w[4][:], w[5][:], ALU.mult))
                    v.tensor_scalar_add(w[7][:], w[6][:], 1e-5)
                    vch.sync(v, v.tensor_scalar_add(w[8][:], w[6][:], 1e-6))
                    v.reciprocal(w[7][:], w[7][:])
                    vch.sync(v, v.reciprocal(w[8][:], w[8][:]))
                    v.nop().then_inc(s_vst.h, 1)
                    v.wait_ge(s_sqr.h, t + 1)
                    rstd = g.st_rstd[:, t:t + 1]
                    vch.sync(v, v.tensor_tensor(w[3][:], w[2][:], rstd, ALU.mult))
                    vch.sync(v, v.tensor_scalar_mul(g.st_nmr[:, t:t + 1], w[3][:], -1.0))
                    v.tensor_tensor(zsc[:], rstd, g.st_r[:, t:t + 1], ALU.mult)
                    vch.sync(v, v.tensor_tensor(zbi[:], g.st_nmr[:, t:t + 1],
                                                g.st_r[:, t:t + 1], ALU.mult))
                    v.nop().then_inc(s_vst.h, 1)
                    if t >= 2:
                        v.wait_ge(s_zf.h, 2 * t - 2)
                    v.tensor_scalar(z2bf[t % 2][:], x_[:], zsc[:], zbi[:],
                                    ALU.mult, ALU.add).then_inc(s_zv.h, 1)
                    if P0LVL < 3:
                        continue
                    v.wait_ge(s_kvf.h, 2 * t + 1)
                    pk = psk[t % 2]
                    v.tensor_copy(kf0[:], pk[:]).then_inc(s_kn.h, 1)
                    v.wait_ge(s_kms.h, t + 1)
                    vch.sync(v, v.tensor_scalar(kms[:], kms[:], 1.0 / HD, 1e-6,
                                                ALU.mult, ALU.add))
                    v.reciprocal(kms[:], kms[:]).then_inc(s_kn1.h, 1)
                    v.wait_ge(s_kn2.h, t + 1)
                    wdma(v, s_w, 6)
                    for j in range(HKV):
                        sl = slice(j * HD, (j + 1) * HD)
                        vins = v.scalar_tensor_tensor(kf[:, sl], kf0[:, sl], krms[:, j:j + 1],
                                                      knwr[:, sl], ALU.mult, ALU.mult)
                    if P0LVL < 4:
                        continue
                    vch.sync(v, vins)
                    kf3 = kf[:].rearrange("p (h two q) -> p h two q", h=HKV, two=2)
                    kr3 = krot[:].rearrange("p (h two q) -> p h two q", h=HKV, two=2)
                    v.tensor_scalar_mul(kr3[:, :, 0, :], kf3[:, :, 1, :],
                                        -1.0).then_inc(s_kfr.h, 1)
                    vch.sync(v, v.tensor_copy(kr3[:, :, 1, :], kf3[:, :, 0, :]))
                    wdma(v, s_cs, 2 * t + 2)
                    v.tensor_tensor(kt1[:], kf[:], ck[t % 3][:], ALU.mult)
                    vch.sync(v, v.tensor_tensor(krot[:], krot[:], sk[t % 3][:], ALU.mult))
                    v.tensor_tensor(krope[:], kt1[:], krot[:], ALU.add).then_inc(s_kro.h, 1)
                    v.wait_ge(s_ksq.h, t + 1)
                    vch.sync(v, v.tensor_tensor(g.kmax2[:], g.kmax2[:], kms2t[:],
                                                ALU.max))
                    v.nop().then_inc(s_kmx.h, 1)

            @b.scalar
            def _(sc):
                for t in range(NT):
                    wdma(sc, s_x, t + 1)
                    sc.activation(sq[:], xb[t % 4][:], AF.Square,
                                  accum_out=w[1][:]).then_inc(s_sq1.h, 1)
                    sc.wait_ge(s_vst.h, 2 * t + 1)
                    sc.activation(g.st_rstd[:, t:t + 1], w[5][:], AF.Sqrt)
                    sc.activation(g.st_r[:, t:t + 1], w[8][:], AF.Sqrt)
                    sch.sync(sc, sc.activation(g.st_s[:, t:t + 1], w[7][:], AF.Sqrt))
                    sc.nop().then_inc(s_sqr.h, 1)
                    sc.wait_ge(s_vst.h, 2 * t + 2)
                    if t >= 2:
                        wdma(sc, g.s_xnb, t - 1)
                    x_ = xb[t % 4]
                    sc.activation(xnbf[t % 2][:], x_[:], AF.Identity,
                                  bias=g.st_nmr[:, t:t + 1],
                                  scale=g.st_rstd[:, t:t + 1]).then_inc(s_act.h, 1)
                    sch.sync(sc, sc.activation(xnchk[:], xnbf[t % 2][:, 0:1], AF.Copy))
                    sc.dma_start(g.xnbf_d[t * P:(t + 1) * P, :],
                                 xnbf[t % 2][:]).then_inc(g.s_xnb.h, 16)
                    if P0LVL < 2:
                        continue
                    wdma(sc, s_w, 6)
                    sc.wait_ge(s_zf.h, 2 * t + 1)
                    for j in range(2):
                        ins = sc.activation(z2T[:, j * 512:(j + 1) * 512],
                                            psZa[:, j * 512:(j + 1) * 512], AF.Copy)
                    ins.then_inc(s_zd.h, 1)
                    sc.wait_ge(s_zf.h, 2 * t + 2)
                    for j in range(2):
                        ins = sc.activation(z2T[:, 1024 + j * 512:1024 + (j + 1) * 512],
                                            psZb[:, j * 512:(j + 1) * 512], AF.Copy)
                    ins.then_inc(s_zd.h, 1)
                    if P0LVL < 3:
                        continue
                    sc.wait_ge(s_kn.h, t + 1)
                    for j in range(HKV):
                        sl = slice(j * HD, (j + 1) * HD)
                        ins = sc.activation(sq[:, sl], kf0[:, sl], AF.Square,
                                            accum_out=kms[:, j:j + 1])
                    ins.then_inc(s_kms.h, 1)
                    sc.wait_ge(s_kn1.h, t + 1)
                    sc.activation(krms[:], kms[:], AF.Sqrt).then_inc(s_kn2.h, 1)
                    sc.wait_ge(s_kfr.h, t + 1)
                    if t >= 1:
                        sc.wait_ge(s_kmx.h, t)
                    for j in range(HKV):
                        sl = slice(j * HD, (j + 1) * HD)
                        ins = sc.activation(sq[:, sl], kf[:, sl], AF.Square,
                                            accum_out=kms2t[:, j:j + 1])
                    ins.then_inc(s_ksq.h, 1)
                    if P0LVL < 3:
                        continue
                    sc.wait_ge(s_kvf.h, 2 * t + 2)
                    sc.activation(g.v_sb[:, t * NKV:(t + 1) * NKV], psv[t % 2][:],
                                  AF.Copy).then_inc(s_vd.h, 1)
                    if P0LVL < 4:
                        continue
                    sc.wait_ge(s_ktf.h, t + 1)
                    for j in range(HKV):
                        ins = sc.activation(g.kT_sb[:, j * L + t * P: j * L + (t + 1) * P],
                                            psk[t % 2][:, j * P:(j + 1) * P], AF.Copy)
                    ins.then_inc(s_ktd.h, 1)

            @b.tensor
            def _(t_):
                wdma(t_, s_w, 6)
                t_.wait_ge(s_wsc.h, 1)
                for t in range(NT):
                    if P0LVL < 2:
                        continue
                    t_.wait_ge(s_zv.h, t + 1)
                    if t >= 1:
                        t_.wait_ge(s_zd.h, 2 * t - 1)
                    transpose_blocks(t_, psZa, z2bf[t % 2], g.ident_sb, 8,
                                     src_off=0).then_inc(s_zf.h, 1)
                    if t >= 1:
                        t_.wait_ge(s_zd.h, 2 * t)
                    transpose_blocks(t_, psZb, z2bf[t % 2], g.ident_sb, 8,
                                     src_off=8 * P).then_inc(s_zf.h, 1)
                    if P0LVL < 3:
                        continue
                    t_.wait_ge(s_zd.h, 2 * t + 2)
                    if t >= 2:
                        t_.wait_ge((s_ktd if P0LVL >= 4 else s_kn).h, t - 1)
                    for dt in range(DT):
                        ins = t_.matmul(psk[t % 2][:], z2T[:, dt * P:(dt + 1) * P],
                                        kw_sb[:, dt * NKV:(dt + 1) * NKV],
                                        start=(dt == 0), stop=(dt == DT - 1))
                    ins.then_inc(s_kvf.h, 1)
                    if t >= 2:
                        t_.wait_ge(s_vd.h, t - 1)
                    for dt in range(DT):
                        ins = t_.matmul(psv[t % 2][:], z2T[:, dt * P:(dt + 1) * P],
                                        vw_sb[:, dt * NKV:(dt + 1) * NKV],
                                        start=(dt == 0), stop=(dt == DT - 1))
                    ins.then_inc(s_kvf.h, 1)
                    if P0LVL < 4:
                        continue
                    t_.wait_ge(s_kn.h, t + 1)
                    t_.wait_ge(s_kro.h, t + 1)
                    transpose_blocks(t_, psk[t % 2], krope, g.ident_sb,
                                     4).then_inc(s_ktf.h, 1)



def build():
    nc = bass.Bass("TRN2", target_bir_lowering=False, detect_race_conditions=False)
    g = declare_io(nc)
    with ExitStack() as ctx:
        e = ctx.enter_context
        sb = lambda n, s, d: e(nc.sbuf_tensor(n, list(s), d))
        g.st_rstd = sb("st_rstd", (P, NT), F32)
        g.st_nmr = sb("st_nmr", (P, NT), F32)
        g.st_s = sb("st_s", (P, NT), F32)
        g.st_r = sb("st_r", (P, NT), F32)
        g.ident_sb = sb("ident_sb", (P, P), BF16)
        g.identf_sb = sb("identf_sb", (P, P), F32)
        g.cw_sb = sb("cw_sb", (P, NT * E), F32)
        g.kmax2 = sb("kmax2", (P, HKV), F32)
        g.qmax2 = sb("qmax2", (P, HQ), F32)
        g.negC = sb("negC", (P, 1), F32)
        g._clr = None
        g.s_xnb = Sem(g, ctx, "s_xnb")
        g.s_pcw = Sem(g, ctx, "s_pcw")

        with ExitStack() as actx:
            ae = actx.enter_context
            g.kT_sb = ae(nc.sbuf_tensor("kT_sb", [P, HKV * L], BF16))
            g.v_sb = ae(nc.sbuf_tensor("v_sb", [P, NT * NKV], BF16))
            phase0(g, ctx, actx)
            phase1(g, ctx, actx)
            phase2(g, ctx, actx)
        phase3(g, ctx)
        phase4(g, ctx)
        phase5(g, ctx)
        phase6(g, ctx)
    nc.finalize()
    return g


def phase1(g, ctx, actx):
    nc = g.nc
    begin_phase(g)
    with ExitStack() as px:
        pe_ = px.enter_context
        sb = lambda n, s, d: pe_(nc.sbuf_tensor(n, list(s), d))
        ps = lambda n, s: pe_(nc.psum_tensor(n, list(s), F32))
        xq = [sb(f"p1_xq{i}", (P, D), BF16) for i in range(2)]
        z2q = [sb(f"p1_z2q{i}", (P, D), BF16) for i in range(2)]
        z2T = sb("p1_z2T", (P, D), BF16)
        qw_sb = sb("p1_qw", (P, DT * D), BF16)
        anw = sb("p1_anw", (P, DT), F32)
        qnwr = sb("p1_qnwr", (P, D), BF16)
        cq = sb("p1_cq", (P, 1024), BF16)
        sq_ = sb("p1_sq", (P, 1024), BF16)
        qf = sb("p1_qf", (P, 1024), F32)
        qrot = sb("p1_qrot", (P, 1024), F32)
        qt1 = sb("p1_qt1", (P, 1024), F32)
        qms = sb("p1_qms", (P, DT), F32)
        qrms = sb("p1_qrms", (P, DT), F32)
        sqs = sb("p1_sqs", (P, P), F32)
        qtile = [sb(f"p1_qtile{i}", (P, D), BF16) for i in range(2)]
        qms2t = sb("p1_qms2t", (P, 8), F32)
        onesr1 = sb("p1_ones1", (1, P), BF16)
        qmxh = sb("p1_qmxh", (P, 1), F32)
        kmxh = sb("p1_kmxh", (P, 1), F32)
        c2q = sb("p1_c2q", (P, 1), F32)
        c2k = sb("p1_c2k", (P, 1), F32)
        cbf = sb("p1_cbf", (P, 1), BF16)
        psA = ps("p1_psA", (P, 1024))
        psB = ps("p1_psB", (P, 1024))

        s_xq = Sem(g, px, "p1_xq")
        s_cq = Sem(g, ctx, "p1_cq", persist=True)
        s_w1 = Sem(g, ctx, "p1_w1", persist=True)
        s_z2q = Sem(g, px, "p1_z2q")
        s_tz = Sem(g, px, "p1_tz")
        s_ze = Sem(g, px, "p1_ze")
        s_qf = Sem(g, px, "p1_qf")
        s_qms = Sem(g, px, "p1_qms")
        s_qn1 = Sem(g, px, "p1_qn1")
        s_qn2 = Sem(g, px, "p1_qn2")
        s_qd = Sem(g, px, "p1_qd")
        s_tq = Sem(g, px, "p1_tq")
        s_qe = Sem(g, px, "p1_qe")
        s_qs = Sem(g, px, "p1_qs")
        s_qfr = Sem(g, px, "p1_qfr")
        s_qsq = Sem(g, px, "p1_qsq")
        s_qmx = Sem(g, px, "p1_qmx")
        s_ca = Sem(g, px, "p1_ca")
        s_cb = Sem(g, px, "p1_cb")
        s_wsc1 = Sem(g, px, "p1_wsc")
        vch = Chain(g, px, "p1_vch")
        sch = Chain(g, px, "p1_sch")
        emit_clears(g)
        with nc.Block("p1") as b:
            @b.sync
            def _(sp):
                for tau in range(NT):
                    if tau >= 2:
                        sp.wait_ge(s_z2q.h, tau - 1)
                    sp.dma_start(xq[tau % 2][:],
                                 g.xnbf_d[tau * P:(tau + 1) * P, :]).then_inc(s_xq.h, 16)
                    sp.wait_ge(s_qe.h, 2 * tau + 2)
                    if tau >= 2:
                        wdma(sp, s_qs, tau - 1)
                    sp.dma_start(
                        g.qT_d[:, :, tau * P:(tau + 1) * P].rearrange("h p c -> p h c"),
                        qtile[tau % 2][:].rearrange("p (h c) -> p h c", h=HQ),
                    ).then_inc(s_qs.h, 16)
                wdma(sp, s_qs, NT)

            @b.gpsimd
            def _(gp):
                qw3 = wload3(g.qwT)
                qsb3 = sb3(qw_sb[:], DT)
                for q4 in range(4):
                    gp.dma_start(qsb3[:, q4 * 4:(q4 + 1) * 4, :],
                                 qw3[:, q4 * 4:(q4 + 1) * 4, :]).then_inc(s_w1.h, 16)
                gp.dma_start(anw[:], g.anw_c).then_inc(s_w1.h, 16)
                gp.dma_start(qnwr[:], g.qnw_r).then_inc(s_w1.h, 16)
                for tau in range(NT):
                    for hf in range(2):
                        step = 2 * tau + hf
                        if step >= 1:
                            gp.wait_ge(s_qd.h, step)
                        hsl0 = hf * 1024
                        gp.dma_start(cq[:], g.cosq[tau * P:(tau + 1) * P,
                                                   hsl0:hsl0 + 1024]).then_inc(s_cq.h, 16)
                        gp.dma_start(sq_[:], g.sinq[tau * P:(tau + 1) * P,
                                                    hsl0:hsl0 + 1024]).then_inc(s_cq.h, 16)

            @b.vector
            def _(v):
                wdma(v, s_w1, 6)
                v.memset(g.qmax2[:], 0.0)
                v.memset(onesr1[:], 1.0)
                v.tensor_scalar_mul(qnwr[:], qnwr[:], SCALE ** 0.5)
                for dt in range(DT):
                    vins = v.tensor_scalar_mul(qw_sb[:, dt * D:(dt + 1) * D],
                                               qw_sb[:, dt * D:(dt + 1) * D],
                                               anw[:, dt:dt + 1])
                vch.sync(v, vins)
                v.nop().then_inc(s_wsc1.h, 1)
                for tau in range(NT):
                    for hf in range(2):
                        step = 2 * tau + hf + 1
                        pshf = psA if hf == 0 else psB
                        v.wait_ge(s_qms.h, 2 * tau + hf + 1)
                        vch.sync(v, v.tensor_scalar(qrms[:, hf * 8:hf * 8 + 8],
                                                    qms[:, hf * 8:hf * 8 + 8],
                                                    1.0 / HD, 1e-6, ALU.mult, ALU.add))
                        v.reciprocal(qrms[:, hf * 8:hf * 8 + 8],
                                     qrms[:, hf * 8:hf * 8 + 8]).then_inc(s_qn1.h, 1)
                        v.wait_ge(s_qn2.h, 2 * tau + hf + 1)
                        if step >= 2:
                            v.wait_ge(s_qsq.h, step - 1)
                            hfp = (step - 2) % 2
                            vch.sync(v, v.tensor_tensor(
                                g.qmax2[:, hfp * 8:hfp * 8 + 8],
                                g.qmax2[:, hfp * 8:hfp * 8 + 8],
                                qms2t[:, 0:8], ALU.max))
                            v.nop().then_inc(s_qmx.h, 1)
                        for j in range(8):
                            sl = slice(j * P, (j + 1) * P)
                            habs = hf * 8 + j
                            vins = v.scalar_tensor_tensor(
                                qf[:, sl], pshf[:, sl], qrms[:, habs:habs + 1],
                                qnwr[:, habs * P:(habs + 1) * P], ALU.mult, ALU.mult)
                        vch.sync(v, vins)
                        qf3 = qf[:].rearrange("p (h two q) -> p h two q", h=8, two=2)
                        qr3 = qrot[:].rearrange("p (h two q) -> p h two q", h=8, two=2)
                        v.tensor_scalar_mul(qr3[:, :, 0, :], qf3[:, :, 1, :],
                                            -1.0).then_inc(s_qfr.h, 1)
                        vch.sync(v, v.tensor_copy(qr3[:, :, 1, :], qf3[:, :, 0, :]))
                        wdma(v, s_cq, 2 * (2 * tau + hf + 1))
                        hsl = slice(hf * 1024, hf * 1024 + 1024)
                        v.tensor_tensor(qt1[:], qf[:], cq[:], ALU.mult)
                        vch.sync(v, v.tensor_tensor(qrot[:], qrot[:], sq_[:], ALU.mult))
                        if tau >= 1:
                            v.wait_ge(s_tq.h, 2 * (tau - 1) + hf + 1)
                        v.tensor_tensor(z2q[tau % 2][:, hsl], qt1[:], qrot[:],
                                        ALU.add).then_inc(s_qd.h, 1)
                v.wait_ge(s_qsq.h, 2 * NT)
                vch.sync(v, v.tensor_tensor(g.qmax2[:, 8:16], g.qmax2[:, 8:16],
                                            qms2t[:, 0:8], ALU.max))
                v.nop().then_inc(s_qmx.h, 1)
                v.wait_ge(s_ca.h, 1)
                vch.sync(v, v.reduce_max(qmxh[0:16, 0:1], psA[0:16, 0:128], axis=AX.X))
                vch.sync(v, v.reduce_max(kmxh[0:4, 0:1], psB[0:4, 0:128], axis=AX.X))
                v.nop().then_inc(s_cb.h, 1)
                v.wait_ge(s_ca.h, 2)
                vch.sync(v, v.reduce_max(c2q[0:1, 0:1], psA[0:1, 0:16], axis=AX.X))
                vch.sync(v, v.reduce_max(c2k[0:1, 0:1], psB[0:1, 0:4], axis=AX.X))
                vch.sync(v, v.tensor_tensor(c2q[0:1, 0:1], c2q[0:1, 0:1],
                                            c2k[0:1, 0:1], ALU.mult))
                v.nop().then_inc(s_cb.h, 1)

            @b.scalar
            def _(sc):
                wdma(sc, s_w1, 6)
                for tau in range(NT):
                    wdma(sc, s_xq, tau + 1)
                    if tau >= 2:
                        sc.wait_ge(s_tq.h, 2 * tau - 2)
                    sc.activation(z2q[tau % 2][:], xq[tau % 2][:], AF.Identity,
                                  scale=g.st_r[:, tau:tau + 1]).then_inc(s_z2q.h, 1)
                    sc.wait_ge(s_tz.h, 2 * tau + 1)
                    for j in range(2):
                        ins = sc.activation(z2T[:, j * 512:(j + 1) * 512],
                                            psA[:, j * 512:(j + 1) * 512], AF.Copy)
                    ins.then_inc(s_ze.h, 1)
                    sc.wait_ge(s_tz.h, 2 * tau + 2)
                    for j in range(2):
                        ins = sc.activation(z2T[:, 1024 + j * 512:1024 + (j + 1) * 512],
                                            psB[:, j * 512:(j + 1) * 512], AF.Copy)
                    ins.then_inc(s_ze.h, 1)
                    for hf in range(2):
                        pshf = psA if hf == 0 else psB
                        sc.wait_ge(s_qf.h, 2 * tau + hf + 1)
                        for j in range(8):
                            ins = sc.activation(sqs[:],
                                                pshf[:, j * P:(j + 1) * P], AF.Square,
                                                accum_out=qms[:, hf * 8 + j:hf * 8 + j + 1])
                        ins.then_inc(s_qms.h, 1)
                        sc.wait_ge(s_qn1.h, 2 * tau + hf + 1)
                        sch.sync(sc, sc.activation(qrms[:, hf * 8:hf * 8 + 8],
                                                   qrms[:, hf * 8:hf * 8 + 8],
                                                   AF.Sqrt))
                        sc.nop().then_inc(s_qn2.h, 1)
                    for hf in range(2):
                        step = 2 * tau + hf + 1
                        sc.wait_ge(s_qfr.h, step)
                        if step >= 2:
                            sc.wait_ge(s_qmx.h, step - 1)
                        for j in range(8):
                            sl = slice(j * P, (j + 1) * P)
                            ins = sc.activation(sqs[:], qf[:, sl], AF.Square,
                                                accum_out=qms2t[:, j:j + 1])
                        ins.then_inc(s_qsq.h, 1)
                    for hf in range(2):
                        pshf = psA if hf == 0 else psB
                        sc.wait_ge(s_tq.h, 2 * tau + hf + 1)
                        if tau >= 2 and hf == 0:
                            wdma(sc, s_qs, tau - 1)
                        for j in range(2):
                            ins = sc.activation(
                                qtile[tau % 2][:, hf * 1024 + j * 512:
                                      hf * 1024 + (j + 1) * 512],
                                pshf[:, j * 512:(j + 1) * 512], AF.Copy)
                        ins.then_inc(s_qe.h, 1)
                sc.wait_ge(s_cb.h, 2)
                sch.sync(sc, sc.activation(cbf[0:1, 0:1], c2q[0:1, 0:1], AF.Sqrt))
                sc.nop().then_inc(s_cb.h, 1)
                sc.wait_ge(s_ca.h, 3)
                sch.sync(sc, sc.activation(g.negC[:, 0:1], psA[:, 0:1], AF.Copy,
                                           scale=-1.0))

            @b.tensor
            def _(t_):
                wdma(t_, s_w1, 6)
                t_.wait_ge(s_wsc1.h, 1)
                for tau in range(NT):
                    t_.wait_ge(s_z2q.h, tau + 1)
                    if tau >= 1:
                        t_.wait_ge(s_qe.h, 2 * tau - 1)
                    transpose_blocks(t_, psA, z2q[tau % 2], g.ident_sb, 8,
                                     src_off=0).then_inc(s_tz.h, 1)
                    if tau >= 1:
                        t_.wait_ge(s_qe.h, 2 * tau)
                    transpose_blocks(t_, psB, z2q[tau % 2], g.ident_sb, 8,
                                     src_off=8 * P).then_inc(s_tz.h, 1)
                    t_.wait_ge(s_ze.h, 2 * tau + 2)
                    for hf in range(2):
                        pshf = psA if hf == 0 else psB
                        for dt in range(DT):
                            for ch in range(2):
                                ins = t_.matmul(
                                    pshf[:, ch * 512:(ch + 1) * 512],
                                    z2T[:, dt * P:(dt + 1) * P],
                                    qw_sb[:, dt * D + hf * 1024 + ch * 512:
                                          dt * D + hf * 1024 + (ch + 1) * 512],
                                    start=(dt == 0), stop=(dt == DT - 1))
                        ins.then_inc(s_qf.h, 1)
                    for hf in range(2):
                        pshf = psA if hf == 0 else psB
                        t_.wait_ge(s_qd.h, 2 * tau + hf + 1)
                        transpose_blocks(t_, pshf, z2q[tau % 2], g.ident_sb, 8,
                                         src_off=hf * 1024).then_inc(s_tq.h, 1)
                t_.wait_ge(s_qe.h, 2 * NT)
                t_.wait_ge(s_qmx.h, 2 * NT)
                t_.matmul(psA[0:16, 0:128], g.qmax2[:], g.identf_sb[:],
                          start=True, stop=True)
                t_.matmul(psB[0:4, 0:128], g.kmax2[:], g.identf_sb[:],
                          start=True, stop=True).then_inc(s_ca.h, 1)
                t_.wait_ge(s_cb.h, 1)
                t_.matmul(psA[0:1, 0:16], qmxh[0:16, 0:1], g.identf_sb[0:16, 0:16],
                          start=True, stop=True)
                t_.matmul(psB[0:1, 0:4], kmxh[0:4, 0:1], g.identf_sb[0:4, 0:4],
                          start=True, stop=True).then_inc(s_ca.h, 1)
                t_.wait_ge(s_cb.h, 3)
                t_.matmul(psA[:, 0:1], onesr1[:], cbf[0:1, 0:1],
                          start=True, stop=True).then_inc(s_ca.h, 1)
    return


def phase2(g, ctx, actx):
    nc = g.nc
    begin_phase(g)
    with ExitStack() as px:
        pe_ = px.enter_context
        sb = lambda n, s, d: pe_(nc.sbuf_tensor(n, list(s), d))
        ps = lambda n, s: pe_(nc.psum_tensor(n, list(s), F32))
        qTb = [sb(f"p2_qTb{i}", (P, HQ * QB), BF16) for i in range(2)]
        mltb = sb("p2_mltb", (P, P), BF16)
        ones_row = sb("p2_ones", (1, P), BF16)
        p2scr = sb("p2_p2scr", (P, 3 * 2048), BF16)
        p2scrT = sb("p2_p2scrT", (P, 2 * 2048), BF16)
        l_all2 = sb("p2_lall", (P, HQ * 32), F32)
        lfin = sb("p2_lfin", (P, 4), F32)
        lbf = [sb(f"p2_lbf{i}", (P, 4), BF16) for i in range(2)]
        lrow_bf = [sb(f"p2_lrow{i}", (1, QB), BF16) for i in range(2)]
        linvb = [sb(f"p2_linvb{i}", (P, QB), BF16) for i in range(2)]
        attn_raw = [sb(f"p2_araw{i}", (P, QB), BF16) for i in range(2)]
        attn_h = [sb(f"p2_attnh{i}", (P, QB), BF16) for i in range(2)]
        pcb = [sb(f"p2_pcb{i}", (P, D), BF16) for i in range(2)]
        psA = ps("p2_psA", (P, 1024))
        psB = ps("p2_psB", (P, 1024))
        psT = [ps(f"p2_psT{i}", (P, 512)) for i in range(2)]
        psO = [ps(f"p2_psO{i}", (P, QB)) for i in range(2)]

        s_pcl = Sem(g, px, "p2_pcl")
        s_qt = Sem(g, px, "p2_qt")
        s_w2 = Sem(g, ctx, "p2_w2b", persist=True)
        s_fill = Sem(g, px, "p2_fill")
        s_mexp = Sem(g, px, "p2_mexp")
        s_trf = Sem(g, px, "p2_trf")
        s_tre = Sem(g, px, "p2_tre")
        s_pv = Sem(g, px, "p2_pv")
        s_lf = Sem(g, px, "p2_lf")
        s_ar = Sem(g, px, "p2_ar")
        s_lt = Sem(g, px, "p2_lt")
        s_lr = Sem(g, px, "p2_lr")
        s_bc = Sem(g, px, "p2_bc")
        s_li = Sem(g, px, "p2_li")
        s_ae = Sem(g, px, "p2_ae")
        s_at = Sem(g, px, "p2_at")
        vch = Chain(g, px, "p2_vch")

        NU = NQB * HQ
        sched = []
        cumF = cumT = cumC = 0
        for qb in range(NQB):
            for h in range(HQ):
                sched.append(dict(qb=qb, h=h, hx=qb * HQ + h, nch=qb + 1,
                                  bF=cumF, bT=cumT, gc0=cumC))
                cumF += 2 * (qb + 1)
                cumT += 4 * (qb + 1)
                cumC += qb + 1
        fills_thru = {}
        run = 0
        for qb in range(NQB):
            run += HQ * 2 * (qb + 1)
            fills_thru[qb] = run

        emit_clears(g)
        with nc.Block("p2") as b:
            @b.sync
            def _(sp):
                def stores(qb):
                    for h in range(HQ):
                        hx = qb * HQ + h
                        sp.wait_ge(s_ae.h, hx + 1)
                        sp.dma_start(g.attnT_d[h, :, qb * QB:(qb + 1) * QB],
                                     attn_h[hx % 2][:]).then_inc(s_at.h, 16)

                for qb in range(NQB):
                    if qb >= 2:
                        sp.wait_ge(s_fill.h, fills_thru[qb - 2])
                    sp.dma_start(
                        qTb[qb % 2][:].rearrange("p (h c) -> p h c", h=HQ),
                        g.qT_d[:, :, qb * QB:(qb + 1) * QB].rearrange("h p c -> p h c"),
                    ).then_inc(s_qt.h, 16)
                    if qb >= 1:
                        stores(qb - 1)
                stores(NQB - 1)
                wdma(sp, s_at, NU)

            @b.gpsimd
            def _(gp):
                gp.dma_start(mltb[:], g.mask_bf).then_inc(s_w2.h, 16)

                pc_ci = [0]

                def pc_chunk():
                    ci = pc_ci[0]
                    if ci >= E * IT:
                        return
                    pc_ci[0] += 1
                    e, ib = divmod(ci, IT)
                    bb = pcb[ci % 2]
                    if ci >= 2:
                        wdma(gp, g.s_pcw, ci - 1)
                    gp.dma_start(bb[:], g.wdT[e, ib * P:(ib + 1) * P, :]
                                 ).then_inc(s_pcl.h, 16)
                    wdma(gp, s_pcl, ci + 1)
                    gp.dma_start(g.wdbf_d[e, ib], bb[:]).then_inc(g.s_pcw.h, 16)

                for u in sched:
                    pc_chunk()
                    gp.wait_ge(s_mexp.h, u["bF"] + 2 * u["nch"])
                while pc_ci[0] < E * IT:
                    pc_chunk()

            @b.tensor
            def _(t_):
                wdma(t_, s_w2, 1)

                def bc(u):
                    t_.wait_ge(s_lr.h, u + 1)
                    t_.matmul(psO[u % 2][:, 0:QB], ones_row[:],
                              lrow_bf[u % 2][0:1, :],
                              start=True, stop=True).then_inc(s_bc.h, 1)

                def lbfT(u):
                    t_.wait_ge(s_lf.h, u + 1)
                    t_.wait_ge(s_ar.h, u + 1)
                    for qt in range(4):
                        ins = t_.matmul(psO[u % 2][0:1, qt * P:(qt + 1) * P],
                                        lbf[u % 2][:, qt:qt + 1], g.ident_sb[:, :],
                                        start=(qt == 0), stop=(qt == 3))
                    ins.then_inc(s_lt.h, 1)

                for u in sched:
                    qb, h, hx, nch = u["qb"], u["h"], u["hx"], u["nch"]
                    bF, bT, gc0 = u["bF"], u["bT"], u["gc0"]
                    kvh = h // 4
                    if h == 0:
                        if hx >= 2:
                            bc(hx - 2)
                        wdma(t_, s_qt, qb + 1)

                    def fills(c):
                        diag = c == qb
                        for pi in range(2):
                            gh = bF + 2 * c + pi + 1
                            if gh > 2:
                                t_.wait_ge(s_mexp.h, gh - 2)
                            pp = psA if pi == 0 else psB
                            for qi, qt in enumerate((2 * pi, 2 * pi + 1)):
                                off = qi * 512
                                qsl = qTb[qb % 2][:, h * QB + qt * P:h * QB + (qt + 1) * P]
                                kb = kvh * L + c * 512
                                if not diag:
                                    ins = t_.matmul(pp[:, off:off + 512], qsl,
                                                    g.kT_sb[:, kb:kb + 512],
                                                    start=True, stop=True)
                                else:
                                    if qt > 0:
                                        t_.matmul(pp[:, off:off + P * qt], qsl,
                                                  g.kT_sb[:, kb:kb + P * qt],
                                                  start=True, stop=True)
                                    t_.matmul(pp[:, off + P * qt:off + P * (qt + 1)],
                                              qsl,
                                              g.kT_sb[:, kb + P * qt:kb + P * (qt + 1)],
                                              start=True, stop=False)
                                    ins = t_.matmul(pp[:, off + P * qt:off + P * (qt + 1)],
                                                    g.ident_sb[:, :], mltb[:],
                                                    start=False, stop=True)
                            ins.then_inc(s_fill.h, 1)

                    def trans(c, ktl):
                        diag = c == qb
                        gt = bT + 4 * c + ktl + 1
                        gc = gc0 + c + 1
                        par = (gc - 1) % 3
                        t_.wait_ge(s_mexp.h, bF + 2 * c + 2)
                        if gt > 2:
                            t_.wait_ge(s_tre.h, gt - 2)
                        q0 = ktl if diag else 0
                        ins = None
                        for qt in range(q0, 4):
                            ins = t_.matmul(
                                psT[(gt - 1) % 2][:, qt * P:(qt + 1) * P],
                                p2scr[:, par * 2048 + qt * 512 + ktl * P:
                                      par * 2048 + qt * 512 + (ktl + 1) * P],
                                g.ident_sb[:, :], start=(qt == q0), stop=(qt == 3))
                        ins.then_inc(s_trf.h, 1)

                    first_pv = [True]

                    def pv(c, ktl):
                        diag = c == qb
                        gt = bT + 4 * c + ktl + 1
                        gc = gc0 + c + 1
                        par = (gc - 1) % 2
                        ktg = 4 * c + ktl
                        qoff = P * ktl if diag else 0
                        t_.wait_ge(s_tre.h, gt)
                        if first_pv[0] and hx >= 2:
                            t_.wait_ge(s_li.h, hx - 1)
                        last = (c == qb and ktl == 3)
                        ins = t_.matmul(
                            psO[hx % 2][:, qoff:QB],
                            g.v_sb[:, ktg * NKV + kvh * HD:ktg * NKV + (kvh + 1) * HD],
                            p2scrT[:, par * 2048 + ktl * 512 + qoff:
                                   par * 2048 + (ktl + 1) * 512],
                            start=first_pv[0], stop=last)
                        first_pv[0] = False
                        ins.then_inc(s_pv.h, 1)

                    fills(0)
                    if nch > 1:
                        fills(1)
                    if hx >= 2 and h != 0:
                        bc(hx - 2)
                    for c in range(1, nch):
                        for ktl in range(4):
                            trans(c - 1, ktl)
                            if ktl >= 1:
                                pv(c - 1, ktl - 1)
                        pv(c - 1, 3)
                        if c + 1 < nch:
                            fills(c + 1)
                    for ktl in range(4):
                        trans(nch - 1, ktl)
                        if ktl >= 1:
                            pv(nch - 1, ktl - 1)
                    pv(nch - 1, 3)
                    if hx >= 1:
                        lbfT(hx - 1)
                bc(NU - 2)
                lbfT(NU - 1)
                bc(NU - 1)

            @b.scalar
            def _(sc):
                for u in sched:
                    qb, h, hx, nch = u["qb"], u["h"], u["hx"], u["nch"]
                    bF, gc0 = u["bF"], u["gc0"]
                    la = l_all2[:, h * 32:(h + 1) * 32]
                    for c in range(nch):
                        diag = c == qb
                        gc = gc0 + c + 1
                        par = (gc - 1) % 3
                        if gc > 3:
                            sc.wait_ge(s_trf.h, 4 * (gc - 3))
                        for pi in range(2):
                            gh = bF + 2 * c + pi + 1
                            sc.wait_ge(s_fill.h, gh)
                            pp = psA if pi == 0 else psB
                            for qi, qt in enumerate((2 * pi, 2 * pi + 1)):
                                off = qi * 512
                                N = 512 if not diag else P * (qt + 1)
                                ins = sc.activation(
                                    p2scr[:, par * 2048 + qt * 512:
                                          par * 2048 + qt * 512 + N],
                                    pp[:, off:off + N], AF.Exp,
                                    bias=g.negC[:, 0:1],
                                    accum_out=la[:, qt * 8 + c:qt * 8 + c + 1])
                            ins.then_inc(s_mexp.h, 1)

            @b.vector
            def _(v):
                wdma(v, s_w2, 1)
                v.memset(ones_row[:], 1.0)

                def ladder_head(u):
                    v.wait_ge(s_bc.h, u + 1)
                    cp = v.tensor_copy(linvb[u % 2][:], psO[u % 2][:, 0:QB])
                    vch.sync(v, cp)
                    v.nop().then_inc(s_li.h, 1)
                    if u >= 2:
                        wdma(v, s_at, u - 1)
                    v.tensor_tensor(attn_h[u % 2][:], attn_raw[u % 2][:],
                                    linvb[u % 2][:], ALU.mult).then_inc(s_ae.h, 1)

                def lfin_block(hxp, nchp, hp):
                    lap = l_all2[:, hp * 32:(hp + 1) * 32]
                    v.wait_ge(s_mexp.h, sched[hxp]["bF"] + 2 * nchp)
                    if hxp >= 1:
                        v.wait_ge(s_lt.h, hxp)
                    for qt in range(4):
                        vins = v.reduce_sum(lfin[:, qt:qt + 1],
                                            lap[:, qt * 8:qt * 8 + nchp], axis=AX.X)
                    vch.sync(v, vins)
                    vch.sync(v, v.reciprocal(lfin[:], lfin[:]))
                    v.tensor_copy(lbf[hxp % 2][:], lfin[:]).then_inc(s_lf.h, 1)

                def raw_block(hxp, nchp):
                    v.wait_ge(s_pv.h, sched[hxp]["bT"] + 4 * nchp)
                    v.tensor_copy(attn_raw[hxp % 2][:],
                                  psO[hxp % 2][:]).then_inc(s_ar.h, 1)

                pend = []
                for u in sched:
                    qb, h, hx, nch = u["qb"], u["h"], u["hx"], u["nch"]
                    bF, bT, gc0 = u["bF"], u["bT"], u["gc0"]
                    if hx >= 2:
                        ladder_head(hx - 2)

                    def evacs(c):
                        diag = c == qb
                        gc = gc0 + c + 1
                        par = (gc - 1) % 2
                        for ktl in range(4):
                            gt = bT + 4 * c + ktl + 1
                            v.wait_ge(s_trf.h, gt)
                            if gc > 2:
                                v.wait_ge(s_pv.h, 4 * (gc - 2))
                            qoff = P * ktl if diag else 0
                            v.tensor_copy(
                                p2scrT[:, par * 2048 + ktl * 512 + qoff:
                                       par * 2048 + (ktl + 1) * 512],
                                psT[(gt - 1) % 2][:, qoff:512]).then_inc(s_tre.h, 1)

                    for c in range(nch):
                        evacs(c)
                        if pend:
                            pend.pop(0)()
                    while pend:
                        pend.pop(0)()
                    pend = [lambda hxp=hx, nchp=nch, hp=h: lfin_block(hxp, nchp, hp),
                            lambda hxp=hx, nchp=nch: raw_block(hxp, nchp)]
                    if hx >= 1:
                        v.wait_ge(s_lt.h, hx)
                        v.tensor_copy(lrow_bf[(hx - 1) % 2][:],
                                      psO[(hx - 1) % 2][0:1, 0:QB]).then_inc(s_lr.h, 1)
                while pend:
                    pend.pop(0)()
                ladder_head(NU - 2)
                v.wait_ge(s_lt.h, NU)
                v.tensor_copy(lrow_bf[(NU - 1) % 2][:],
                              psO[(NU - 1) % 2][0:1, 0:QB]).then_inc(s_lr.h, 1)
                ladder_head(NU - 1)
    return



def phase3(g, ctx):
    nc = g.nc
    begin_phase(g)
    with ExitStack() as px:
        pe_ = px.enter_context
        sb = lambda n, s, d: pe_(nc.sbuf_tensor(n, list(s), d))
        ps = lambda n, s: pe_(nc.psum_tensor(n, list(s), F32))
        at_sb = [sb(f"p3_at{i}", (P, D), BF16) for i in range(2)]
        xb3 = [sb(f"p3_xb{i}", (P, D), F32) for i in range(2)]
        ow_sb = sb("p3_ow", (P, DT * D), BF16)
        outb = sb("p3_outb", (P, D), F32)
        fnw = sb("p3_fnw", (P, DT), F32)
        gw_sb = sb("p3_gw", (P, DT * E), F32)
        xnf = sb("p3_xnf", (P, D), F32)
        h2f = sb("p3_h2f", (P, D), F32)
        h2bf = sb("p3_h2bf", (P, D), BF16)
        h2T = sb("p3_h2T", (P, D), BF16)
        restb = sb("p3_rest", (P, D), F32)
        h2r32 = sb("p3_h2r32", (P, D), F32)
        h2T32 = sb("p3_h2T32", (P, D), F32)
        t1 = sb("p3_t1", (P, 1024), F32)
        sqsc = sb("p3_sqsc", (P, 512), F32)
        hs2 = sb("p3_hs2", (P, 4), F32)
        rrc = sb("p3_rrc", (P, 1), F32)
        g3t = sb("p3_g3t", (P, 1), F32)
        gp0 = sb("p3_gp0", (P, E), F32)
        gnm = sb("p3_gnm", (P, 1), F32)
        gsum = sb("p3_gsum", (P, 1), F32)
        gm1 = sb("p3_gm1", (P, 1), F32)
        gm2 = sb("p3_gm2", (P, 1), F32)
        geq1 = sb("p3_geq1", (P, E), F32)
        gmask = sb("p3_gmask", (P, E), F32)
        gden = sb("p3_gden", (P, 1), F32)
        gw1 = sb("p3_gw1", (P, 1), F32)
        gw2 = sb("p3_gw2", (P, 1), F32)
        ps1 = [ps(f"p3_ps1{i}", (P, 1024)) for i in range(2)]
        ps2 = [ps(f"p3_ps2{i}", (P, 1024)) for i in range(2)]

        s_at3 = Sem(g, px, "p3_at")
        s_x3 = Sem(g, px, "p3_x")
        s_w3 = Sem(g, ctx, "p3_w", persist=True)
        s_of = Sem(g, px, "p3_of")
        s_xn = Sem(g, px, "p3_xn")
        s_h2 = Sem(g, px, "p3_h2")
        s_rd = Sem(g, px, "p3_rd")
        s_rs = Sem(g, px, "p3_rs")
        s_hs = Sem(g, px, "p3_hs")
        s_rr = Sem(g, px, "p3_rr")
        s_hb = Sem(g, px, "p3_hb")
        s_t2 = Sem(g, px, "p3_t2")
        s_e2 = Sem(g, px, "p3_e2")
        s_h2s = Sem(g, px, "p3_h2s")
        s_gf = Sem(g, px, "p3_gf")
        s_gx = Sem(g, px, "p3_gx")
        s_g1 = Sem(g, px, "p3_g1")
        s_gd = Sem(g, px, "p3_gd")
        s_gps = Sem(g, px, "p3_gps")
        s_t3 = Sem(g, px, "p3_t3")
        s_e3 = Sem(g, px, "p3_e3")
        vch = Chain(g, px, "p3_vch")
        sch = Chain(g, px, "p3_sch")

        emit_clears(g)
        with nc.Block("p3") as b:
            @b.sync
            def _(sp):
                for t in range(NT):
                    if t >= 2:
                        sp.wait_ge(s_of.h, 2 * t - 2)
                    sp.dma_start(
                        at_sb[t % 2][:].rearrange("p (j c) -> p j c", j=HQ),
                        g.attnT_d[:, :, t * P:(t + 1) * P].rearrange("j p c -> p j c"),
                    ).then_inc(s_at3.h, 16)
                    if t >= 2:
                        sp.wait_ge(s_rd.h, 2 * t - 2)
                    sp.dma_start(xb3[t % 2][:], g.x[t * P:(t + 1) * P, :]).then_inc(s_x3.h, 16)
                    sp.wait_ge(s_rd.h, 2 * t + 1)
                    sp.dma_start(g.rest_d[t * P:(t + 1) * P, 0:1024],
                                 restb[:, 0:1024]).then_inc(s_rs.h, 16)
                    sp.wait_ge(s_rd.h, 2 * t + 2)
                    sp.dma_start(g.rest_d[t * P:(t + 1) * P, 1024:2048],
                                 restb[:, 1024:2048]).then_inc(s_rs.h, 16)
                    sp.wait_ge(s_e2.h, 2 * t + 2)
                    if t >= 2:
                        wdma(sp, s_h2s, t - 1)
                    sp.dma_start(
                        g.h2T_d[:, :, t * P:(t + 1) * P].rearrange("k p c -> p k c"),
                        h2T[:].rearrange("p (k c) -> p k c", k=DT),
                    ).then_inc(s_h2s.h, 16)
                wdma(sp, s_rs, 2 * NT)
                wdma(sp, s_h2s, NT)
                if DBG:
                    sp.wait_ge(s_gd.h, NT)
                    sp.dma_start(g.dbg_cw.rearrange("t p e -> p t e"),
                                 g.cw_sb[:].rearrange("p (t e) -> p t e", t=NT)).then_inc(s_rs.h, 16)
                    wdma(sp, s_rs, 2 * NT + 1)

            @b.gpsimd
            def _(gp):
                ow3 = wload3(g.owT)
                osb3 = sb3(ow_sb[:], DT)
                for q4 in range(4):
                    gp.dma_start(osb3[:, q4 * 4:(q4 + 1) * 4, :],
                                 ow3[:, q4 * 4:(q4 + 1) * 4, :]).then_inc(s_w3.h, 16)
                gp.dma_start(outb[:], g.outb_r).then_inc(s_w3.h, 16)
                gp.dma_start(fnw[:], g.fnw_c).then_inc(s_w3.h, 16)
                gp.dma_start(sb3(gw_sb[:], DT), wload3(g.gwT)).then_inc(s_w3.h, 16)

            @b.tensor
            def _(t_):
                wdma(t_, s_w3, 7)
                for t in range(NT):
                    wdma(t_, s_at3, t + 1)
                    for hf in range(2):
                        pp = ps1[hf]
                        if t >= 1:
                            t_.wait_ge(s_h2.h, 2 * t - 1 + hf)
                        for jt in range(DT):
                            for ch in range(2):
                                ins = t_.matmul(
                                    pp[:, ch * 512:(ch + 1) * 512],
                                    at_sb[t % 2][:, jt * P:(jt + 1) * P],
                                    ow_sb[:, jt * D + hf * 1024 + ch * 512:
                                          jt * D + hf * 1024 + (ch + 1) * 512],
                                    start=(jt == 0), stop=(jt == DT - 1))
                        ins.then_inc(s_of.h, 1)
                    t_.wait_ge(s_hb.h, t + 1)
                    if t >= 1:
                        t_.wait_ge(s_e3.h, 2 * t)
                        t_.wait_ge(s_e2.h, 2 * t - 1)
                    transpose_blocks(t_, ps2[0], h2bf, g.ident_sb, 8,
                                     src_off=0).then_inc(s_t2.h, 1)
                    if t >= 1:
                        t_.wait_ge(s_e2.h, 2 * t)
                        t_.wait_ge(s_gps.h, t)
                    transpose_blocks(t_, ps2[1], h2bf, g.ident_sb, 8,
                                     src_off=8 * P).then_inc(s_t2.h, 1)
                    t_.wait_ge(s_e2.h, 2 * t + 2)
                    for half in range(2):
                        transpose_blocks(t_, ps2[half], h2r32, g.identf_sb, 8,
                                         src_off=half * 1024).then_inc(s_t3.h, 1)
                    t_.wait_ge(s_e3.h, 2 * t + 2)
                    for dt in range(DT):
                        ins = t_.matmul(ps2[0][:, 0:E],
                                        h2T32[:, dt * P:(dt + 1) * P],
                                        gw_sb[:, dt * E:(dt + 1) * E],
                                        start=(dt == 0), stop=(dt == DT - 1))
                    ins.then_inc(s_gf.h, 1)

            @b.scalar
            def _(sc):
                wdma(sc, s_w3, 7)
                for t in range(NT):
                    wdma(sc, s_x3, t + 1)
                    if t >= 1:
                        sc.wait_ge(s_rd.h, 2 * t)
                    sc.activation(xnf[:], xb3[t % 2][:], AF.Identity,
                                  scale=g.st_rstd[:, t:t + 1],
                                  bias=g.st_nmr[:, t:t + 1]).then_inc(s_xn.h, 1)
                    for hf in range(2):
                        sc.wait_ge(s_h2.h, 2 * t + hf + 1)
                        for cc in range(2):
                            hsl = slice(hf * 1024 + cc * 512, hf * 1024 + (cc + 1) * 512)
                            ins = sc.activation(sqsc[:], h2f[:, hsl], AF.Square,
                                                accum_out=hs2[:, 2 * hf + cc:2 * hf + cc + 1])
                        ins.then_inc(s_hs.h, 1)
                    sc.wait_ge(s_rr.h, t + 1)
                    sch.sync(sc, sc.activation(rrc[:], rrc[:], AF.Sqrt))
                    sch.sync(sc, sc.activation(h2r32[:], h2f[:], AF.Identity,
                                               scale=rrc[:]))
                    sc.activation(h2bf[:], h2r32[:], AF.Copy).then_inc(s_hb.h, 1)
                    for half in range(2):
                        sc.wait_ge(s_t2.h, 2 * t + half + 1)
                        if t >= 1 and half == 0:
                            wdma(sc, s_h2s, t)
                        for j in range(8):
                            dt = half * 8 + j
                            ins = sc.activation(h2T[:, dt * P:(dt + 1) * P],
                                                ps2[half][:, j * P:(j + 1) * P],
                                                AF.Copy, scale=fnw[:, dt:dt + 1])
                        ins.then_inc(s_e2.h, 1)
                    for half in range(2):
                        sc.wait_ge(s_t3.h, 2 * t + half + 1)
                        for j in range(8):
                            dt = half * 8 + j
                            ins = sc.activation(h2T32[:, dt * P:(dt + 1) * P],
                                                ps2[half][:, j * P:(j + 1) * P],
                                                AF.Copy, scale=fnw[:, dt:dt + 1])
                        ins.then_inc(s_e3.h, 1)
                    sc.wait_ge(s_g1.h, t + 1)
                    sc.activation(gp0[:], gp0[:], AF.Exp, bias=gnm[:],
                                  accum_out=gsum[:]).then_inc(s_gx.h, 1)

            @b.vector
            def _(v):
                for t in range(NT):
                    v.wait_ge(s_xn.h, t + 1)
                    for hf in range(2):
                        pp = ps1[hf]
                        hsl = slice(hf * 1024, (hf + 1) * 1024)
                        v.wait_ge(s_of.h, 2 * t + hf + 1)
                        v.tensor_tensor(h2f[:, hsl], pp[:], xnf[:, hsl],
                                        ALU.add).then_inc(s_h2.h, 1)
                        if t >= 1 and hf == 0:
                            wdma(v, s_rs, 2 * t)
                        v.tensor_tensor(t1[:], xb3[t % 2][:, hsl], xnf[:, hsl], ALU.add)
                        vch.sync(v, v.tensor_tensor(t1[:], t1[:], outb[:, hsl], ALU.add))
                        v.tensor_tensor(restb[:, hsl], t1[:], h2f[:, hsl],
                                        ALU.add).then_inc(s_rd.h, 1)
                    v.wait_ge(s_hs.h, 2 * t + 2)
                    v.tensor_tensor(rrc[:], hs2[:, 0:1], hs2[:, 1:2], ALU.add)
                    vch.sync(v, v.tensor_tensor(g3t[:], hs2[:, 2:3], hs2[:, 3:4], ALU.add))
                    vch.sync(v, v.tensor_tensor(rrc[:], rrc[:], g3t[:], ALU.add))
                    vch.sync(v, v.tensor_scalar(rrc[:], rrc[:], 1.0 / D, 1e-6,
                                                ALU.mult, ALU.add))
                    v.reciprocal(rrc[:], rrc[:]).then_inc(s_rr.h, 1)
                    v.wait_ge(s_gf.h, t + 1)
                    vch.sync(v, v.tensor_copy(gp0[:], ps2[0][:, 0:E]))
                    v.reduce_max(gnm[:], gp0[:], axis=AX.X, negate=True).then_inc(s_g1.h, 1)
                    v.wait_ge(s_gx.h, t + 1)
                    vch.sync(v, v.reciprocal(gsum[:], gsum[:]))
                    vch.sync(v, v.tensor_scalar_mul(gp0[:], gp0[:], gsum[:]))
                    v.nop().then_inc(s_gps.h, 1)
                    vch.sync(v, v.reduce_max(gm1[:], gp0[:], axis=AX.X))
                    vch.sync(v, v.tensor_scalar(geq1[:], gp0[:], gm1[:], None, ALU.is_equal))
                    vch.sync(v, v.tensor_tensor(gmask[:], gp0[:], geq1[:], ALU.mult))
                    vch.sync(v, v.tensor_tensor(gmask[:], gp0[:], gmask[:], ALU.subtract))
                    vch.sync(v, v.reduce_max(gm2[:], gmask[:], axis=AX.X))
                    vch.sync(v, v.tensor_tensor(gden[:], gm1[:], gm2[:], ALU.add))
                    vch.sync(v, v.tensor_scalar_add(gden[:], gden[:], 1e-9))
                    vch.sync(v, v.reciprocal(gden[:], gden[:]))
                    v.tensor_tensor(gw1[:], gm1[:], gden[:], ALU.mult)
                    vch.sync(v, v.tensor_tensor(gw2[:], gm2[:], gden[:], ALU.mult))
                    vch.sync(v, v.tensor_scalar(gmask[:], gmask[:], gm2[:], None, ALU.is_equal))
                    vch.sync(v, v.tensor_scalar_mul(geq1[:], geq1[:], gw1[:]))
                    v.scalar_tensor_tensor(g.cw_sb[:, t * E:(t + 1) * E], gmask[:],
                                           gw2[:], geq1[:], ALU.mult, ALU.add).then_inc(s_gd.h, 1)
    return


def phase4(g, ctx):
    nc = g.nc
    LH = L // 2
    TH = LH // P
    CH = 4
    begin_phase(g)
    with ExitStack() as px:
        pe_ = px.enter_context
        sb = lambda n, s, d: pe_(nc.sbuf_tensor(n, list(s), d))
        ps = lambda n, s: pe_(nc.psum_tensor(n, list(s), F32))
        xz = [sb(f"p4_xz{i}", (P, D), BF16) for i in range(2)]
        z1b = [sb(f"p4_z1b{i}", (P, D), BF16) for i in range(2)]
        z1T = sb("p4_z1T", (P, DT * LH), BF16)
        rmsw = sb("p4_rmsw", (P, DT), F32)
        Abc = sb("p4_Abc", (P, DT), F32)
        wA = [sb(f"p4_wA{i}", (P, D), BF16) for i in range(2)]
        wB = [sb(f"p4_wB{i}", (P, D), BF16) for i in range(2)]
        wD = [sb(f"p4_wD{i}", (P, D), BF16) for i in range(2)]
        wCa = [sb(f"p4_wCa{i}", (P, D), BF16) for i in range(2)]
        wCb = [sb(f"p4_wCb{i}", (P, D), BF16) for i in range(2)]
        a_sb = sb("p4_a", (P, LH), F32)
        u_sb = sb("p4_u", (P, LH), F32)
        h_sb = sb("p4_h", (P, LH), F32)
        c_sb = sb("p4_c", (P, LH), F32)
        dtmp = sb("p4_dtmp", (P, 512), F32)
        catmp = sb("p4_catmp", (P, LH), F32)
        ybf = [sb(f"p4_ybf{i}", (P, LH), BF16) for i in range(2)]
        carry = sb("p4_carry", (P, DT), F32)
        psp = [ps(f"p4_ps{i}", (P, 512)) for i in range(5)]

        s_xz = Sem(g, px, "p4_xz")
        s_w4 = Sem(g, ctx, "p4_w4", persist=True)
        s_wg = Sem(g, ctx, "p4_wg", persist=True)
        s_z1 = Sem(g, px, "p4_z1")
        s_tz = Sem(g, px, "p4_tz")
        s_ze = Sem(g, px, "p4_ze")
        s_pf = Sem(g, px, "p4_pf")
        s_ae = Sem(g, px, "p4_ae")
        s_ue = Sem(g, px, "p4_ue")
        s_ce = Sem(g, px, "p4_ce")
        s_cd = Sem(g, px, "p4_cd")
        s_sc = Sem(g, px, "p4_sc")
        s_ys = Sem(g, px, "p4_ys")
        vch = Chain(g, px, "p4_vch")
        sch = Chain(g, px, "p4_sch")

        emit_clears(g)
        with nc.Block("p4") as b:
            @b.sync
            def _(sp):
                for hf in range(2):
                    for tt in range(TH):
                        tau = hf * TH + tt
                        if tau >= 2:
                            sp.wait_ge(s_z1.h, tau - 1)
                        sp.dma_start(xz[tt % 2][:],
                                     g.xnbf_d[tau * P:(tau + 1) * P, :]).then_inc(s_xz.h, 16)
                    for gi in range(DT):
                        gg = hf * DT + gi
                        sp.wait_ge(s_sc.h, gg + 1)
                        if gg >= 2:
                            wdma(sp, s_ys, gg - 1)
                        sp.dma_start(g.yT_d[gi, :, hf * LH:(hf + 1) * LH],
                                     ybf[gg % 2][:]).then_inc(s_ys.h, 16)
                wdma(sp, s_ys, 2 * DT)

            @b.gpsimd
            def _(gp):
                gp.dma_start(rmsw[:], g.rmsw_c).then_inc(s_w4.h, 16)
                gp.dma_start(Abc[:], g.Ab_c).then_inc(s_w4.h, 16)
                for hf in range(2):
                    for gi in range(DT):
                        gg = hf * DT + gi
                        if gg >= 2:
                            gp.wait_ge(s_pf.h, 20 * (gg - 1))
                        cs = slice(gi * P, (gi + 1) * P)
                        for wsb, wdr in ((wA, g.AwT), (wB, g.BwT), (wD, g.DwT),
                                         (wCa, g.CawT), (wCb, g.CbwT)):
                            gp.dma_start(sb3(wsb[gg % 2][:], DT),
                                         wdr[:, cs].rearrange("(k p) n -> p k n", p=P)
                                         ).then_inc(s_wg.h, 16)

            @b.scalar
            def _(sc):
                wdma(sc, s_w4, 2)
                for hf in range(2):
                    if hf == 1:
                        sc.wait_ge(s_ue.h, 64)
                        sc.wait_ge(s_cd.h, 64)
                    for tt in range(TH):
                        tau = hf * TH + tt
                        wdma(sc, s_xz, tau + 1)
                        if tau >= 2:
                            sc.wait_ge(s_tz.h, 4 * tau - 4)
                        sc.activation(z1b[tt % 2][:], xz[tt % 2][:], AF.Identity,
                                      scale=g.st_s[:, tau:tau + 1]).then_inc(s_z1.h, 1)
                        for qtr in range(4):
                            sc.wait_ge(s_tz.h, 4 * tau + qtr + 1)
                            for j in range(4):
                                dt = qtr * 4 + j
                                ins = sc.activation(
                                    z1T[:, dt * LH + tt * P:dt * LH + (tt + 1) * P],
                                    psp[qtr][:, j * P:(j + 1) * P],
                                    AF.Copy, scale=rmsw[:, dt:dt + 1])
                            ins.then_inc(s_ze.h, 1)
                    for gi in range(DT):
                        gg = hf * DT + gi
                        for tc in range(CH):
                            cu = gg * CH + tc
                            if tc == 0 and gg >= 1:
                                sc.wait_ge(s_sc.h, gg)
                            sc.wait_ge(s_pf.h, 5 * cu + 1)
                            sc.activation(a_sb[:, tc * 512:(tc + 1) * 512],
                                          psp[0][:], AF.Sigmoid,
                                          bias=Abc[:, gi:gi + 1]).then_inc(s_ae.h, 1)
                            if cu >= 1:
                                sc.wait_ge(s_ue.h, cu)
                            sc.wait_ge(s_pf.h, 5 * cu + 3)
                            sc.activation(dtmp[:], psp[2][:],
                                          AF.Sigmoid).then_inc(s_ae.h, 1)
                            if cu >= 4:
                                sc.wait_ge(s_cd.h, cu - 3)
                            sc.wait_ge(s_pf.h, 5 * cu + 4)
                            sc.activation(catmp[:, tc * 512:(tc + 1) * 512],
                                          psp[3][:], AF.Silu).then_inc(s_ce.h, 1)

            @b.vector
            def _(v):
                for hf in range(2):
                    for gi in range(DT):
                        gg = hf * DT + gi
                        for tc in range(CH):
                            cu = gg * CH + tc
                            v.wait_ge(s_ae.h, 2 * cu + 2)
                            v.tensor_tensor(u_sb[:, tc * 512:(tc + 1) * 512],
                                            dtmp[:], psp[1][:],
                                            ALU.mult).then_inc(s_ue.h, 1)
                            v.wait_ge(s_ce.h, cu + 1)
                            v.wait_ge(s_pf.h, 5 * cu + 5)
                            v.tensor_tensor(c_sb[:, tc * 512:(tc + 1) * 512],
                                            catmp[:, tc * 512:(tc + 1) * 512],
                                            psp[4][:],
                                            ALU.mult).then_inc(s_cd.h, 1)
                        init = 0.0 if hf == 0 else carry[:, gi:gi + 1]
                        scn = v.tensor_tensor_scan(h_sb[:], a_sb[:], u_sb[:], init,
                                                   ALU.mult, ALU.add)
                        if hf == 0:
                            vch.sync(v, scn)
                            v.tensor_copy(carry[:, gi:gi + 1], h_sb[:, LH - 1:LH])
                        if gg >= 2:
                            wdma(v, s_ys, gg - 1)
                        v.tensor_tensor(ybf[gg % 2][:], c_sb[:], h_sb[:],
                                        ALU.mult).then_inc(s_sc.h, 1)

            @b.tensor
            def _(t_):
                for hf in range(2):
                    if hf == 1:
                        t_.wait_ge(s_ae.h, 2 * 64)
                        t_.wait_ge(s_ue.h, 64)
                        t_.wait_ge(s_ce.h, 64)
                        t_.wait_ge(s_cd.h, 64)
                    for tt in range(TH):
                        tau = hf * TH + tt
                        t_.wait_ge(s_z1.h, tau + 1)
                        for qtr in range(4):
                            if 4 * tau + qtr >= 4:
                                t_.wait_ge(s_ze.h, 4 * tau + qtr - 3)
                            transpose_blocks(t_, psp[qtr], z1b[tt % 2], g.ident_sb, 4,
                                             src_off=qtr * 4 * P).then_inc(s_tz.h, 1)
                    for gi in range(DT):
                        gg = hf * DT + gi
                        if gi == 0:
                            t_.wait_ge(s_ze.h, 64 * (hf + 1))
                        wdma(t_, s_wg, 5 * (gg + 1))
                        for tc in range(CH):
                            cu = gg * CH + tc
                            for pj, wsb in ((0, wA), (1, wB), (2, wD)):
                                if cu >= 1:
                                    if pj == 0:
                                        t_.wait_ge(s_ae.h, 2 * cu - 1)
                                    elif pj == 1:
                                        t_.wait_ge(s_ue.h, cu)
                                    else:
                                        t_.wait_ge(s_ae.h, 2 * cu)
                                for dt in range(DT):
                                    ins = t_.matmul(
                                        psp[pj][:],
                                        wsb[gg % 2][:, dt * P:(dt + 1) * P],
                                        z1T[:, dt * LH + tc * 512:dt * LH + (tc + 1) * 512],
                                        start=(dt == 0), stop=(dt == DT - 1))
                                ins.then_inc(s_pf.h, 1)
                            if cu >= 1:
                                t_.wait_ge(s_ce.h, cu)
                                t_.wait_ge(s_cd.h, cu)
                            for pj, wsb in ((3, wCa), (4, wCb)):
                                for dt in range(DT):
                                    ins = t_.matmul(
                                        psp[pj][:],
                                        wsb[gg % 2][:, dt * P:(dt + 1) * P],
                                        z1T[:, dt * LH + tc * 512:dt * LH + (tc + 1) * 512],
                                        start=(dt == 0), stop=(dt == DT - 1))
                                ins.then_inc(s_pf.h, 1)
    return


def phase5(g, ctx):
    nc = g.nc
    LH = L // 2
    begin_phase(g)
    with ExitStack() as px:
        pe_ = px.enter_context
        sb = lambda n, s, d: pe_(nc.sbuf_tensor(n, list(s), d))
        ps = lambda n, s: pe_(nc.psum_tensor(n, list(s), F32))
        ow5 = sb("p5_ow", (P, DT * D), BF16)
        yt_sb = sb("p5_yt", (P, DT * LH), BF16)
        rb = [sb(f"p5_rb{i}", (P, D), F32) for i in range(2)]
        r2b = [sb(f"p5_r2b{i}", (P, D), F32) for i in range(2)]
        ps5 = [ps(f"p5_ps{i}", (P, D)) for i in range(2)]

        s_w5 = Sem(g, ctx, "p5_w", persist=True)
        s_yt = Sem(g, px, "p5_yt")
        s_r1 = Sem(g, px, "p5_r1")
        s_f5 = Sem(g, px, "p5_f")
        s_d5 = Sem(g, px, "p5_d")
        s_rs2 = Sem(g, px, "p5_rs2")

        emit_clears(g)
        with nc.Block("p5") as b:
            @b.sync
            def _(sp):
                for hf in range(2):
                    if hf == 1:
                        sp.wait_ge(s_f5.h, 16)
                    for q4 in range(4):
                        sp.dma_start(
                            yt_sb[:].rearrange("p (k t) -> p k t", k=DT)[:, q4 * 4:(q4 + 1) * 4, :],
                            g.yT_d[q4 * 4:(q4 + 1) * 4, :, hf * LH:(hf + 1) * LH].rearrange("k p t -> p k t"),
                        ).then_inc(s_yt.h, 16)
                    for tt in range(16):
                        t = hf * 16 + tt
                        if t >= 2:
                            sp.wait_ge(s_d5.h, t - 1)
                        sp.dma_start(rb[t % 2][:],
                                     g.rest_d[t * P:(t + 1) * P, :]).then_inc(s_r1.h, 16)
                        sp.wait_ge(s_d5.h, t + 1)
                        if t >= 2:
                            wdma(sp, s_rs2, t - 1)
                        sp.dma_start(g.rest2_d[t * P:(t + 1) * P, :],
                                     r2b[t % 2][:]).then_inc(s_rs2.h, 16)
                wdma(sp, s_rs2, NT)

            @b.gpsimd
            def _(gp):
                ow3 = wload3(g.outwT)
                osb3 = sb3(ow5[:], DT)
                for q4 in range(4):
                    gp.dma_start(osb3[:, q4 * 4:(q4 + 1) * 4, :],
                                 ow3[:, q4 * 4:(q4 + 1) * 4, :]).then_inc(s_w5.h, 16)

            @b.tensor
            def _(t_):
                wdma(t_, s_w5, 4)
                for hf in range(2):
                    wdma(t_, s_yt, 4 * (hf + 1))
                    for tt in range(16):
                        t = hf * 16 + tt
                        pp = ps5[t % 2]
                        if t >= 2:
                            t_.wait_ge(s_d5.h, t - 1)
                        for gi in range(DT):
                            for ch in range(4):
                                ins = t_.matmul(
                                    pp[:, ch * 512:(ch + 1) * 512],
                                    yt_sb[:, gi * LH + tt * P:gi * LH + (tt + 1) * P],
                                    ow5[:, gi * D + ch * 512:gi * D + (ch + 1) * 512],
                                    start=(gi == 0), stop=(gi == DT - 1))
                        ins.then_inc(s_f5.h, 1)

            @b.vector
            def _(v):
                for t in range(NT):
                    v.wait_ge(s_f5.h, t + 1)
                    wdma(v, s_r1, t + 1)
                    if t >= 2:
                        wdma(v, s_rs2, t - 1)
                    v.tensor_tensor(r2b[t % 2][:], ps5[t % 2][:], rb[t % 2][:],
                                    ALU.add).then_inc(s_d5.h, 1)
    return


def phase6(g, ctx):
    nc = g.nc
    TQ = 8
    NEQ = NQ * E
    begin_phase(g)
    with ExitStack() as px:
        pe_ = px.enter_context
        sb = lambda n, s, d: pe_(nc.sbuf_tensor(n, list(s), d))
        ps = lambda n, s: pe_(nc.psum_tensor(n, list(s), F32))
        h2q = sb("p6_h2q", (P, DT * QL), BF16)
        wg6 = [sb(f"p6_wg{i}", (P, D), BF16) for i in range(2)]
        wu6 = [sb(f"p6_wu{i}", (P, D), BF16) for i in range(2)]
        wd6 = sb("p6_wd", (P, IT * D), BF16)
        sg = sb("p6_sg", (P, QL), BF16)
        act6 = [sb(f"p6_act{i}", (P, IT * QL), BF16) for i in range(2)]
        acc = sb("p6_acc", (P, TQ * D), F32)
        psG = ps("p6_psG", (P, QL))
        psU = ps("p6_psU", (P, QL))
        psD = [ps(f"p6_psD{i}", (P, QL)) for i in range(2)]

        s_h2q = Sem(g, px, "p6_h2q")
        s_acc = Sem(g, px, "p6_acc")
        s_wg6 = Sem(g, ctx, "p6_wg6", persist=True)
        s_wu6 = Sem(g, ctx, "p6_wu6", persist=True)
        s_wd = Sem(g, ctx, "p6_wd", persist=True)
        s_gf = Sem(g, px, "p6_gf")
        s_uf = Sem(g, px, "p6_uf")
        s_sil = Sem(g, px, "p6_sil")
        s_a6 = Sem(g, px, "p6_a6")
        s_eof = Sem(g, px, "p6_eof")
        s_ev = Sem(g, px, "p6_ev")
        s_ys6 = Sem(g, px, "p6_ys")

        def oidx(peq, tb, dh):
            return 16 * peq + 2 * tb + dh + 1

        emit_clears(g)
        with nc.Block("p6") as b:
            @b.sync
            def _(sp):
                for Q in range(NQ):
                    if Q >= 1:
                        sp.wait_ge(s_uf.h, E * IT * Q)
                    h3 = h2q[:].rearrange("p (k t) -> p k t", k=DT)
                    for q4 in range(2):
                        sp.dma_start(
                            h3[:, q4 * 4:(q4 + 1) * 4, :],
                            g.h2T_d[q4 * 4:(q4 + 1) * 4, :, Q * QL:(Q + 1) * QL]
                            .rearrange("k p t -> p k t"),
                        ).then_inc(s_h2q.h, 16)
                    if Q >= 1:
                        sp.wait_ge(s_ev.h, 16 * E * Q)
                        sp.dma_start(
                            g.y[(Q - 1) * QL:Q * QL, :].rearrange("(t p) d -> p t d", p=P),
                            acc[:].rearrange("p (t d) -> p t d", t=TQ),
                        ).then_inc(s_ys6.h, 16)
                    sp.dma_start(
                        acc[:].rearrange("p (t d) -> p t d", t=TQ),
                        g.rest2_d[Q * QL:(Q + 1) * QL, :]
                        .rearrange("(t p) d -> p t d", p=P),
                    ).then_inc(s_acc.h, 16)
                    for e in range(E):
                        eq = Q * E + e
                        if eq == 0:
                            wdma(sp, g.s_pcw, E * IT)
                        else:
                            sp.wait_ge(s_eof.h, 16 * eq)
                        sp.dma_start(
                            sb3(wd6[:], IT)[:, 6:IT, :],
                            g.wdbf_d[e, 6:IT, :, :].rearrange("k p n -> p k n"),
                        ).then_inc(s_wd.h, 16)
                sp.wait_ge(s_ev.h, 16 * E * NQ)
                sp.dma_start(
                    g.y[(NQ - 1) * QL:NQ * QL, :].rearrange("(t p) d -> p t d", p=P),
                    acc[:].rearrange("p (t d) -> p t d", t=TQ),
                ).then_inc(s_ys6.h, 16)
                wdma(sp, s_ys6, NQ)

            @b.gpsimd
            def _(gp):
                h3g = h2q[:].rearrange("p (k t) -> p k t", k=DT)
                for Q in range(NQ):
                    if Q >= 1:
                        gp.wait_ge(s_uf.h, E * IT * Q)
                    gp.dma_start(
                        h3g[:, 8:12, :],
                        g.h2T_d[8:12, :, Q * QL:(Q + 1) * QL]
                        .rearrange("k p t -> p k t"),
                    ).then_inc(s_h2q.h, 16)
                    for e in range(E):
                        for it in range(IT):
                            git = (Q * E + e) * IT + it + 1
                            if git >= 3:
                                gp.wait_ge(s_gf.h, git - 2)
                            gp.dma_start(wg6[(git - 1) % 2][:]
                                         .rearrange("p (k n) -> p k n", k=DT),
                                         g.wgT[e, :, it * P:(it + 1) * P]
                                         .rearrange("(k p) n -> p k n", p=P)
                                         ).then_inc(s_wg6.h, 16)
                            if git >= 3:
                                gp.wait_ge(s_uf.h, git - 2)
                            gp.dma_start(wu6[(git - 1) % 2][:]
                                         .rearrange("p (k n) -> p k n", k=DT),
                                         g.wuT[e, :, it * P:(it + 1) * P]
                                         .rearrange("(k p) n -> p k n", p=P)
                                         ).then_inc(s_wu6.h, 16)

            @b.tensor
            def _(t_):
                def eo(peq, tb, dh):
                    pd = psD[dh]
                    if tb == 0 and dh == 0:
                        t_.wait_ge(s_a6.h, IT * (peq + 1))
                        wdma(t_, s_wd, 2 * (peq + 1))
                    oi = oidx(peq, tb, dh)
                    if oi > 2:
                        t_.wait_ge(s_ev.h, oi - 2)
                    ins = None
                    for it in range(IT):
                        for c2 in range(2):
                            ins = t_.matmul(
                                pd[:, c2 * 512:(c2 + 1) * 512],
                                act6[peq % 2][:, it * QL + tb * P:it * QL + (tb + 1) * P],
                                wd6[:, it * D + dh * 1024 + c2 * 512:
                                    it * D + dh * 1024 + (c2 + 1) * 512],
                                start=(it == 0), stop=(it == IT - 1))
                    ins.then_inc(s_eof.h, 1)

                for Q in range(NQ):
                    wdma(t_, s_h2q, 4 * (Q + 1))
                    for e in range(E):
                        eq = Q * E + e
                        for it in range(IT):
                            git = eq * IT + it + 1
                            wdma(t_, s_wg6, git)
                            if git >= 2:
                                t_.wait_ge(s_sil.h, git - 1)
                            ins = None
                            for dt in range(DT):
                                for ch in range(2):
                                    ins = t_.matmul(
                                        psG[:, ch * 512:(ch + 1) * 512],
                                        wg6[(git - 1) % 2][:, dt * P:(dt + 1) * P],
                                        h2q[:, dt * QL + ch * 512:dt * QL + (ch + 1) * 512],
                                        start=(dt == 0), stop=(dt == DT - 1))
                            ins.then_inc(s_gf.h, 1)
                            wdma(t_, s_wu6, git)
                            if git >= 2:
                                t_.wait_ge(s_a6.h, git - 1)
                            ins = None
                            for dt in range(DT):
                                for ch in range(2):
                                    ins = t_.matmul(
                                        psU[:, ch * 512:(ch + 1) * 512],
                                        wu6[(git - 1) % 2][:, dt * P:(dt + 1) * P],
                                        h2q[:, dt * QL + ch * 512:dt * QL + (ch + 1) * 512],
                                        start=(dt == 0), stop=(dt == DT - 1))
                            ins.then_inc(s_uf.h, 1)
                            if eq >= 1 and it < TQ:
                                eo(eq - 1, it, 0)
                                eo(eq - 1, it, 1)
                for tb in range(TQ):
                    eo(NEQ - 1, tb, 0)
                    eo(NEQ - 1, tb, 1)

            @b.scalar
            def _(sc):
                h3s = h2q[:].rearrange("p (k t) -> p k t", k=DT)
                for Q in range(NQ):
                    if Q >= 1:
                        sc.wait_ge(s_uf.h, E * IT * Q)
                    sc.dma_start(
                        h3s[:, 12:16, :],
                        g.h2T_d[12:16, :, Q * QL:(Q + 1) * QL]
                        .rearrange("k p t -> p k t"),
                    ).then_inc(s_h2q.h, 16)
                    for e in range(E):
                        eq = Q * E + e
                        for it in range(IT):
                            git = eq * IT + it + 1
                            if it == TQ:
                                if eq == 0:
                                    wdma(sc, g.s_pcw, E * IT)
                                else:
                                    sc.wait_ge(s_eof.h, 16 * eq)
                                sc.dma_start(
                                    sb3(wd6[:], IT)[:, 0:6, :],
                                    g.wdbf_d[e, 0:6, :, :].rearrange("k p n -> p k n"),
                                ).then_inc(s_wd.h, 16)
                            sc.wait_ge(s_gf.h, git)
                            if git >= 2:
                                sc.wait_ge(s_a6.h, git - 1)
                            sc.activation(sg[:], psG[:], AF.Silu).then_inc(s_sil.h, 1)

            @b.vector
            def _(v):
                def evac(peq, tb):
                    Qp, ep = divmod(peq, E)
                    if ep == 0 and tb == 0:
                        wdma(v, s_acc, Qp + 1)
                    for dh in range(2):
                        oi = oidx(peq, tb, dh)
                        v.wait_ge(s_eof.h, oi)
                        cwc = g.cw_sb[:, (Qp * TQ + tb) * E + ep:
                                      (Qp * TQ + tb) * E + ep + 1]
                        v.scalar_tensor_tensor(
                            acc[:, tb * D + dh * 1024:tb * D + (dh + 1) * 1024],
                            psD[dh][:], cwc,
                            acc[:, tb * D + dh * 1024:tb * D + (dh + 1) * 1024],
                            ALU.mult, ALU.add).then_inc(s_ev.h, 1)

                for Q in range(NQ):
                    for e in range(E):
                        eq = Q * E + e
                        for it in range(IT):
                            git = eq * IT + it + 1
                            v.wait_ge(s_uf.h, git)
                            v.wait_ge(s_sil.h, git)
                            if it == 0 and eq >= 2:
                                v.wait_ge(s_eof.h, 16 * (eq - 1))
                            v.tensor_tensor(act6[eq % 2][:, it * QL:(it + 1) * QL],
                                            sg[:], psU[:], ALU.mult).then_inc(s_a6.h, 1)
                            if eq >= 1 and it < TQ:
                                evac(eq - 1, it)
                for tb in range(TQ):
                    evac(NEQ - 1, tb)
    return



def _get_built():
    if "g" not in _CACHE:
        _CACHE["g"] = build()
    return _CACHE["g"]


def kernel(**inputs):
    g = _get_built()
    sh = _CACHE.get("sh")
    if sh is None or _CACHE.get("sh_id") != id(inputs["L0_attn_A_w"]):
        sh = _prep_shared(inputs)
        _CACHE["sh"] = sh
        _CACHE["sh_id"] = id(inputs["L0_attn_A_w"])
    x = inputs["L0_attn_x"]
    in_maps = []
    for c in range(8):
        m = dict(sh)
        m["x"] = np.ascontiguousarray(x[c])
        in_maps.append(m)
    res = run_bass_kernel_spmd(g.nc, in_maps, core_ids=list(range(8)))
    out = np.stack([np.asarray(r["y"]) for r in res.results], axis=0)
    return out
